# version 13
# baseline (speedup 1.0000x reference)
import numpy as np
import ml_dtypes
import concourse.bass as bass
import concourse.mybir as mybir
from concourse.bass_utils import run_bass_kernel_spmd

F32 = mybir.dt.float32
BF16 = mybir.dt.bfloat16
AF = mybir.ActivationFunctionType
ALU = mybir.AluOpType

B, T, C = 4, 2048, 2048
NCH = 16
FCH = 64
HF = 8192
NTOK = 1024
SUB = 512
NSUB = 2
KC = 4
LNEPS = 1e-5
EPS2 = 1e-24
LNX_STRIDE = 1028

ENGINES = ("tensor", "vector", "scalar", "gpsimd", "sync")
SEM_OF = {"tensor": "PE", "vector": "DV", "scalar": "SC", "gpsimd": "GP", "sync": "SP"}
DMA_SEM_OF = {"gpsimd": "GLD", "sync": "SLD"}


def _hadamard(n):
    i = np.arange(n, dtype=np.int64)
    x = i[:, None] & i[None, :]
    pc = np.zeros((n, n), dtype=np.int64)
    while x.any():
        pc += x & 1
        x >>= 1
    return np.where(pc % 2 == 0, 1.0, -1.0)


def _stage_w(W, oc, ic):
    W = np.asarray(W, dtype=np.float32)
    return np.ascontiguousarray(W.reshape(oc, 128, ic, 128).transpose(0, 3, 2, 1))


class Plan:
    def __init__(self):
        self.ops = {e: [] for e in ENGINES}
        self.cnt = {s: 0 for s in ("PE", "DV", "SC", "GP", "SP", "GLD", "SLD", "ST")}
        self.track = {}

    def _waits_for(self, reads, writes, extra):
        w = {}

        def add(d):
            if d is None:
                return
            s, v = d
            w[s] = max(w.get(s, 0), v)
        for k in reads:
            rec = self.track.get(k)
            if rec:
                add(rec[0])
        for k in writes:
            rec = self.track.get(k)
            if rec:
                add(rec[0])
                for d in rec[1]:
                    add(d)
        for d in extra:
            add(d)
        return w

    def _commit(self, dep, reads, writes):
        for k in reads:
            rec = self.track.setdefault(k, [None, []])
            rec[1].append(dep)
        for k in writes:
            self.track[k] = [dep, []]

    def retire(self, *prefixes):
        deps = []
        dead = []
        for k, rec in self.track.items():
            if any(k[0] == p for p in prefixes):
                if rec[0] is not None:
                    deps.append(rec[0])
                deps.extend(rec[1])
                dead.append(k)
        for k in dead:
            del self.track[k]
        return deps

    @staticmethod
    def _psum_exclusive(reads, writes):
        rd, wr = [], list(writes)
        for k in reads:
            (wr if (isinstance(k[0], str) and k[0].startswith("pb")) else rd).append(k)
        return rd, wr

    def op(self, eng, fn, reads=(), writes=(), extra=()):
        reads, writes = self._psum_exclusive(reads, writes)
        sem = SEM_OF[eng]
        w = self._waits_for(reads, writes, extra)
        wt = tuple(sorted(w.items()))
        self.cnt[sem] += 1
        dep = (sem, self.cnt[sem])

        def run(E, _fn=fn, _wt=wt, _sem=sem, _self=self):
            for (s, v) in _wt:
                E.wait_ge(_self.sems[s], v)
            _fn(E).then_inc(_self.sems[_sem], 1)
        self.ops[eng].append(run)
        self._commit(dep, reads, writes)
        return dep

    def dma(self, eng, fn, reads=(), writes=(), extra=()):
        sem = DMA_SEM_OF[eng]
        w = self._waits_for(reads, writes, extra)
        wt = tuple(sorted(w.items()))
        self.cnt[sem] += 16
        dep = (sem, self.cnt[sem])

        def run(E, _fn=fn, _wt=wt, _sem=sem, _self=self):
            for (s, v) in _wt:
                E.wait_ge(_self.sems[s], v)
            _fn(E).then_inc(_self.sems[_sem], 16)
        self.ops[eng].append(run)
        self._commit(dep, reads, writes)
        return dep

    def store(self, fn, reads=(), extra=()):
        w = self._waits_for(reads, (), extra)
        wt = tuple(sorted(w.items()))
        self.cnt["ST"] += 16
        dep = ("ST", self.cnt["ST"])

        def run(E, _fn=fn, _wt=wt, _self=self):
            for (s, v) in _wt:
                E.wait_ge(_self.sems[s], v)
            _fn(E).then_inc(_self.sems["ST"], 16)
        self.ops["sync"].append(run)
        self._commit(dep, reads, ())
        return dep


def build_nc(phases=99, p3_chunks=NCH, p3_depth=99, debug=False):
    nc = bass.Bass("TRN2", target_bir_lowering=False)

    d_x = nc.dram_tensor("x16", [NCH, 128, NTOK], F32, kind="ExternalInput").ap()
    d_xhalo = nc.dram_tensor("xhalo", [NCH, 128, 3], F32, kind="ExternalInput").ap()
    d_mask = nc.dram_tensor("cmask", [8, 1], F32, kind="ExternalInput").ap()
    d_cin = nc.dram_tensor("ccin", [NCH, 128, 1], F32)
    d_gout = nc.dram_tensor("ccgout", [NCH, 1024, 1], F32)
    d_wq = nc.dram_tensor("wq", [NCH, 128, C], F32, kind="ExternalInput").ap()
    d_wk = nc.dram_tensor("wk", [NCH, 128, C], F32, kind="ExternalInput").ap()
    d_wv = nc.dram_tensor("wv", [NCH, 128, C], F32, kind="ExternalInput").ap()
    d_wg = nc.dram_tensor("wg", [NCH, 128, C], F32, kind="ExternalInput").ap()
    d_wo = nc.dram_tensor("wo", [NCH, 128, C], F32, kind="ExternalInput").ap()
    d_w1 = nc.dram_tensor("w1", [FCH, 128, C], F32, kind="ExternalInput").ap()
    d_w3 = nc.dram_tensor("w3", [FCH, 128, C], F32, kind="ExternalInput").ap()
    d_w2 = nc.dram_tensor("w2", [NCH, 128, HF], F32, kind="ExternalInput").ap()
    d_cwv = nc.dram_tensor("cwv", [128, NCH * KC], F32, kind="ExternalInput").ap()
    d_cbv = nc.dram_tensor("cbv", [128, NCH], F32, kind="ExternalInput").ap()
    d_gbv = nc.dram_tensor("gbv", [128, NCH], F32, kind="ExternalInput").ap()
    d_lnw = nc.dram_tensor("lnw", [128, NCH], F32, kind="ExternalInput").ap()
    d_lnb = nc.dram_tensor("lnb", [128, NCH], F32, kind="ExternalInput").ap()
    d_gam = nc.dram_tensor("gam", [1, NCH], F32, kind="ExternalInput").ap()
    d_t2x = nc.dram_tensor("t2x", [2, 256], F32, kind="ExternalInput").ap()
    d_c3 = nc.dram_tensor("c3", [128, 4], BF16, kind="ExternalInput").ap()
    d_h128 = nc.dram_tensor("h128", [128, 128], BF16, kind="ExternalInput").ap()
    d_out = nc.dram_tensor("out16", [NCH, 128, NTOK], F32, kind="ExternalOutput").ap()
    if debug:
        dbg_x = nc.dram_tensor("dbg_x", [128, NCH * NTOK], F32, kind="ExternalOutput").ap()
        dbg_lnxb = nc.dram_tensor("dbg_lnxb", [128, NCH * LNX_STRIDE], BF16, kind="ExternalOutput").ap()
        dbg_xcv = nc.dram_tensor("dbg_xcv", [128, NCH * NTOK], BF16, kind="ExternalOutput").ap()
        dbg_slotA = nc.dram_tensor("dbg_slotA", [128, NCH * NTOK], BF16, kind="ExternalOutput").ap()
        dbg_carrym = nc.dram_tensor("dbg_carrym", [128, NCH], F32, kind="ExternalOutput").ap()
        dbg_gbc = nc.dram_tensor("dbg_gbc", [128, NCH], F32, kind="ExternalOutput").ap()
        dbg_kb = nc.dram_tensor("dbg_kb", [128, SUB], BF16, kind="ExternalOutput").ap()
        dbg_vb = nc.dram_tensor("dbg_vb", [128, SUB], BF16, kind="ExternalOutput").ap()
        dbg_f2 = nc.dram_tensor("dbg_f2", [128, SUB], F32, kind="ExternalOutput").ap()
        dbg_f3 = nc.dram_tensor("dbg_f3", [128, SUB], F32, kind="ExternalOutput").ap()
        dbg_R = nc.dram_tensor("dbg_R", [128, 2048], F32, kind="ExternalOutput").ap()

    P = Plan()

    from contextlib import ExitStack
    with ExitStack() as ctx:
        ent = ctx.enter_context
        x_sb = ent(nc.sbuf_tensor("x_sb", [128, NCH * NTOK], F32))
        lnxb = ent(nc.sbuf_tensor("lnxb", [128, NCH * LNX_STRIDE], BF16))
        xcv = ent(nc.sbuf_tensor("xcv", [128, NCH * NTOK], BF16))
        slotA = ent(nc.sbuf_tensor("slotA", [128, NCH * NTOK], BF16))
        qb = ent(nc.sbuf_tensor("qb", [128, NTOK], BF16))
        gb = ent(nc.sbuf_tensor("gb", [128, NTOK], BF16))
        kb = ent(nc.sbuf_tensor("kb", [128, NTOK], BF16))
        vb = ent(nc.sbuf_tensor("vb", [128, NTOK], BF16))
        sq0 = ent(nc.sbuf_tensor("sq0", [128, SUB], BF16))
        sq1 = ent(nc.sbuf_tensor("sq1", [128, SUB], BF16))
        f0 = ent(nc.sbuf_tensor("f0", [128, SUB], F32))
        f1 = ent(nc.sbuf_tensor("f1", [128, SUB], F32))
        f2 = ent(nc.sbuf_tensor("f2", [128, SUB], F32))
        f3 = ent(nc.sbuf_tensor("f3", [128, SUB], F32))
        gbc = ent(nc.sbuf_tensor("gbc", [128, NCH], F32))
        gb2 = ent(nc.sbuf_tensor("gb2", [128, NCH], F32))
        cw_sb = ent(nc.sbuf_tensor("cw_sb", [128, NCH * KC], F32))
        cb_sb = ent(nc.sbuf_tensor("cb_sb", [128, NCH], F32))
        carrym = ent(nc.sbuf_tensor("carrym", [128, NCH], F32))
        ccol = ent(nc.sbuf_tensor("ccol", [128, 1], F32))
        epsc = ent(nc.sbuf_tensor("epsc", [128, 1], F32))
        gtmp = ent(nc.sbuf_tensor("gtmp", [1, NCH], F32))
        onec = ent(nc.sbuf_tensor("onec", [128, 1], F32))
        gsb = ent(nc.sbuf_tensor("gsb", [8, 128], F32))
        mkc = ent(nc.sbuf_tensor("mkc", [8, 1], F32))
        xhal = ent(nc.sbuf_tensor("xhal", [128, NCH * 3], F32))
        ccar = ent(nc.sbuf_tensor("ccar", [128, 1], F32))
        ssc = ent(nc.sbuf_tensor("ssc", [128, SUB], F32))
        ctmp = ent(nc.sbuf_tensor("ctmp", [128, 1], F32))
        onesf1 = ent(nc.sbuf_tensor("onesf1", [1, 128], F32))
        R = ent(nc.sbuf_tensor("R", [128, 2048], F32))
        lnwc = ent(nc.sbuf_tensor("lnwc", [128, NCH], F32))
        lnbc = ent(nc.sbuf_tensor("lnbc", [128, NCH], F32))
        t2x_sb = ent(nc.sbuf_tensor("t2x_sb", [2, 256], mybir.dt.float32r))
        c3_sb = ent(nc.sbuf_tensor("c3_sb", [128, 4], BF16))
        h128_sb = ent(nc.sbuf_tensor("h128_sb", [128, 128], BF16))
        wpool = ent(nc.sbuf_tensor("wpool", [128, 4 * 2048], BF16))
        pb = [ent(nc.psum_tensor(f"pb{i}", [128, SUB], F32)) for i in range(8)]

        sems = {}
        for s in ("PE", "DV", "SC", "GP", "SP", "GLD", "SLD", "ST"):
            sems[s] = ent(nc.semaphore(s))
        P.sems = sems

        r_mu = R[0:1, 0:512]
        r_msq = R[0:1, 512:1024]
        r_rstd = R[0:1, 1024:1536]
        r_nm = R[0:1, 1536:2048]
        rrn = [R[0:2, 1024:1536], R[0:2, 1536:2048]]

        def rr32(ap):
            return ap.bitcast(mybir.dt.float32r)

        ones1 = t2x_sb[0:1, 0:128]
        selq = t2x_sb[:, 0:128]
        selk = t2x_sb[:, 128:256]
        Vsum = c3_sb[:, 1:2]
        V0 = c3_sb[:, 1:3]
        V1 = c3_sb[:, 0:2]

        def lnx_view(h, s, k=3, w=SUB):
            base = h * LNX_STRIDE + 4 + s * SUB - 3 + k
            return lnxb[:, base: base + w]

        def yb_view(h, s):
            return lnxb[:, h * NTOK + s * SUB: h * NTOK + (s + 1) * SUB]

        def lnx2_view(h, s):
            return xcv[:, h * NTOK + s * SUB: h * NTOK + (s + 1) * SUB]

        xcv_view = lnx2_view

        def slotA_view(h, s):
            return slotA[:, h * NTOK + s * SUB: h * NTOK + (s + 1) * SUB]

        def lnxdup_view(h, s):
            return slotA[:, s * (NCH * SUB) + h * SUB: s * (NCH * SUB) + (h + 1) * SUB]

        def dupx_view(h, s):
            return xcv[:, s * (NCH * SUB) + h * SUB: s * (NCH * SUB) + (h + 1) * SUB]

        def hsb_view(f, s=None):
            if f < 32:
                return slotA[:, f * SUB: (f + 1) * SUB]
            return lnxb[:, (f - 32) * SUB: (f - 31) * SUB]

        def x_view(h, s):
            return x_sb[:, h * NTOK + s * SUB: h * NTOK + (s + 1) * SUB]

        wctr = [0]

        def wload(dram_ap):
            slot = wctr[0] % 4
            wctr[0] += 1
            key = ("wsl", slot)
            view = wpool[:, slot * 2048:(slot + 1) * 2048]
            P.dma("gpsimd", lambda E, view=view, dram_ap=dram_ap: E.dma_start(out=view, in_=dram_ap),
                  writes=[key])
            return view, key

        P.dma("sync", lambda E: E.dma_start(
            out=x_sb[:, :].rearrange("p (c t) -> p c t", t=NTOK),
            in_=d_x.rearrange("c p t -> p c t")),
            writes=[("x", h, s) for h in range(NCH) for s in range(NSUB)])
        P.dma("sync", lambda E: E.dma_start(out=cw_sb[:, :], in_=d_cwv), writes=[("cw",)])
        P.dma("sync", lambda E: E.dma_start(out=cb_sb[:, :], in_=d_cbv), writes=[("cb",)])
        P.dma("sync", lambda E: E.dma_start(out=gb2[:, :], in_=d_gbv), writes=[("gb2",)])
        P.dma("sync", lambda E: E.dma_start(out=lnwc[:, :], in_=d_lnw), writes=[("lnwc",)])
        P.dma("sync", lambda E: E.dma_start(out=lnbc[:, :], in_=d_lnb), writes=[("lnbc",)])
        P.dma("gpsimd", lambda E: E.dma_start(out=t2x_sb[:, :], in_=d_t2x), writes=[("t2x",)])
        P.dma("sync", lambda E: E.dma_start(out=c3_sb[:, :], in_=d_c3), writes=[("c3",)])
        P.dma("sync", lambda E: E.dma_start(out=h128_sb[:, :], in_=d_h128), writes=[("h128",)])
        P.dma("sync", lambda E: E.dma_start(out=gtmp[:, :], in_=d_gam), writes=[("gtmp",)])
        P.op("vector", lambda E: E.memset(onesf1[:, :], 1.0), writes=[("onesf1",)])
        P.op("vector", lambda E: E.memset(onec[:, :], 1.0), writes=[("onec",)])
        P.dma("sync", lambda E: E.dma_start(
            out=xhal[:, :].rearrange("p (c t) -> p c t", t=3),
            in_=d_xhalo.rearrange("c p t -> p c t")),
            writes=[("xhal",)])
        P.dma("sync", lambda E: E.dma_start(out=mkc[:, :], in_=d_mask), writes=[("mkc",)])

        P.op("gpsimd", lambda E: E.collective_compute(
            "AllGather", ALU.bypass, replica_groups=[list(range(8))],
            ins=[d_cin[0].opt()], outs=[d_gout[0].opt()]))
        P.op("vector", lambda E: E.memset(epsc[:, :], LNEPS), writes=[("epsc",)])
        P.op("tensor", lambda E: E.matmul(pb[4][:, 0:NCH], onesf1[:, :], gtmp[:, :], start=True, stop=True),
             reads=[("onesf1",), ("gtmp",)], writes=[("pb4",)])
        P.op("vector", lambda E: E.tensor_copy(gbc[:, :], pb[4][:, 0:NCH]),
             reads=[("pb4",)], writes=[("gbc",)])
        P.op("vector", lambda E: E.tensor_scalar_mul(gb2[:, :], gb2[:, :], -1.0),
             reads=[("gb2",)], writes=[("gb2",)])

        def ln_rows(bank_sum_key, bank_sq_key, bank_sum, bank_sq, slot=0):
            mu, rstd = r_mu, r_rstd
            msq, nm = r_msq, r_nm
            kmu, krstd = ("mu",), ("rstd",)
            kmsq, knm = ("msq",), ("nm",)
            P.op("vector", lambda E: E.tensor_scalar_mul(rr32(mu), bank_sum[0:1, :], 1.0 / C),
                 reads=[bank_sum_key], writes=[kmu])
            P.op("vector", lambda E: E.tensor_scalar_mul(rr32(msq), bank_sq[0:1, :], 1.0 / C),
                 reads=[bank_sq_key], writes=[kmsq])
            P.op("vector", lambda E: E.tensor_tensor(rr32(rstd), mu, mu, ALU.mult),
                 reads=[kmu], writes=[krstd])
            P.op("vector", lambda E: E.tensor_tensor(rr32(msq), msq, rstd, ALU.subtract),
                 reads=[kmsq, krstd], writes=[kmsq])
            P.op("scalar", lambda E: E.activation(rr32(rstd), msq, AF.Ln, bias=epsc[0:1, :], scale=1.0),
                 reads=[kmsq, ("epsc",)], writes=[krstd])
            P.op("scalar", lambda E: E.activation(rr32(rstd), rstd, AF.Exp, bias=0.0, scale=-0.5),
                 reads=[krstd], writes=[krstd])
            P.op("vector", lambda E: E.tensor_tensor(rr32(nm), mu, rstd, ALU.mult),
                 reads=[kmu, krstd], writes=[knm])
            P.op("vector", lambda E: E.tensor_scalar_mul(rr32(nm), nm, -1.0),
                 reads=[knm], writes=[knm])
            return rstd, nm, krstd, knm

        def emit_ln_stats(src_view, src_key, s, banks, cast=True):
            bsum, bsq, ksum, ksq_ = banks
            for h in range(NCH):
                xv = src_view(h, s)
                xqv = (f0 if h % 2 == 0 else f1).bitcast(BF16)[:, 0:SUB]
                kxq = ("f0",) if h % 2 == 0 else ("f1",)
                if cast:
                    xb = sq0 if h % 2 == 0 else sq1
                    kxb = ("sq0",) if h % 2 == 0 else ("sq1",)
                    P.op("scalar", lambda E, xv=xv, xb=xb: E.copy(xb[:, :], xv),
                         reads=[src_key(h, s)], writes=[kxb])
                    rhs_s, krhs = (lambda xb=xb: xb[:, :])(), kxb
                else:
                    rhs_s, krhs = xv, src_key(h, s)
                P.op("vector", lambda E, xv=xv, xqv=xqv: E.tensor_tensor(xqv, xv, xv, ALU.mult),
                     reads=[src_key(h, s)], writes=[kxq])
                P.op("tensor", lambda E, rhs_s=rhs_s, bsum=bsum, first=(h == 0), last=(h == NCH - 1):
                     E.matmul(bsum[0:1, :], Vsum, rhs_s, start=first, stop=last,
                              skip_group_check=True),
                     reads=[krhs, ("c3",)], writes=[ksum])
                P.op("tensor", lambda E, xqv=xqv, bsq=bsq, first=(h == 0), last=(h == NCH - 1):
                     E.matmul(bsq[0:1, :], Vsum, xqv, start=first, stop=last,
                              skip_group_check=True),
                     reads=[kxq], writes=[ksq_])

        def emit_ln_tail(src_view, src_key, out_view, out_key, s, banks):
            bsum, bsq, ksum, ksq_ = banks
            rstd, nm, krstd, knm = ln_rows(ksum, ksq_, bsum, bsq)
            P.op("tensor", lambda E, rstd=rstd, bsum=bsum: E.matmul(bsum[:, :], rr32(ones1), rr32(rstd), start=True, stop=True),
                 reads=[krstd, ("t2x",)], writes=[ksum])
            P.op("tensor", lambda E, nm=nm, bsq=bsq: E.matmul(bsq[:, :], rr32(ones1), rr32(nm), start=True, stop=True),
                 reads=[knm], writes=[ksq_])
            P.op("scalar", lambda E, bsum=bsum: E.copy(f2[:, :], bsum[:, :]), reads=[ksum], writes=[("f2",)])
            P.op("scalar", lambda E, bsq=bsq: E.copy(f3[:, :], bsq[:, :]), reads=[ksq_], writes=[("f3",)])
            cpA_ln = qb.bitcast(F32)[:, 0:SUB]
            for h in range(NCH):
                xv = src_view(h, s)
                ov = out_view(h, s)
                eng = "gpsimd" if h % 3 == 2 else "vector"
                tmp = f1 if eng == "vector" else cpA_ln
                ktmp = ("f1",) if eng == "vector" else ("cpA",)
                P.op(eng, lambda E, xv=xv, tmp=tmp: E.tensor_tensor(tmp[:, :], xv, f2[:, :], ALU.mult),
                     reads=[src_key(h, s), ("f2",)], writes=[ktmp])
                P.op(eng, lambda E, ov=ov, tmp=tmp: E.tensor_tensor(ov, tmp[:, :], f3[:, :], ALU.add),
                     reads=[ktmp, ("f3",)], writes=[out_key(h, s)])

        LNB = {0: (pb[6], pb[7], ("pb6",), ("pb7",)), 1: (pb[4], pb[5], ("pb4",), ("pb5",))}
        LNB2 = (pb[2], pb[3], ("pb2",), ("pb3",))

        def emit_ln(src_view, src_key, out_view, out_key):
            for s2 in range(NSUB):
                emit_ln_stats(src_view, src_key, s2, LNB[s2])
            for s2 in range(NSUB):
                emit_ln_tail(src_view, src_key, out_view, out_key, s2, LNB[s2])

        x_src = (lambda h, s2: x_view(h, s2), lambda h, s2: ("x", h, s2))
        lnx_dst = (lambda h, s2: lnx_view(h, s2), lambda h, s2: ("lnx", h, s2))
        if phases >= 1:
            emit_ln_stats(*x_src, 0, LNB[0])
            emit_ln_stats(*x_src, 1, LNB[1])
        if phases >= 2:
            for h in range(NCH):
                xv3 = xhal[:, h * 3:(h + 1) * 3]
                xb3 = sq0[:, 0:3]
                xq3 = sq1[:, 0:3]
                P.op("scalar", lambda E, xv3=xv3, xb3=xb3: E.copy(xb3, xv3),
                     reads=[("xhal",)], writes=[("sq0",)])
                P.op("vector", lambda E, xv3=xv3, xq3=xq3: E.tensor_tensor(xq3, xv3, xv3, ALU.mult),
                     reads=[("xhal",)], writes=[("sq1",)])
                P.op("tensor", lambda E, xb3=xb3, first=(h == 0), last=(h == NCH - 1):
                     E.matmul(pb[2][0:1, 0:3], Vsum, xb3, start=first, stop=last,
                              skip_group_check=True),
                     reads=[("sq0",), ("c3",)], writes=[("pb2",)])
                P.op("tensor", lambda E, xq3=xq3, first=(h == 0), last=(h == NCH - 1):
                     E.matmul(pb[3][0:1, 0:3], Vsum, xq3, start=first, stop=last,
                              skip_group_check=True),
                     reads=[("sq1",)], writes=[("pb3",)])
            mu3, rstd3 = r_mu[:, 0:3], r_rstd[:, 0:3]
            msq3, nm3 = r_msq[:, 0:3], r_nm[:, 0:3]
            P.op("vector", lambda E: E.tensor_scalar_mul(rr32(mu3), pb[2][0:1, 0:3], 1.0 / C),
                 reads=[("pb2",)], writes=[("mu",)])
            P.op("vector", lambda E: E.tensor_scalar_mul(rr32(msq3), pb[3][0:1, 0:3], 1.0 / C),
                 reads=[("pb3",)], writes=[("msq",)])
            P.op("vector", lambda E: E.tensor_tensor(rr32(rstd3), mu3, mu3, ALU.mult),
                 reads=[("mu",)], writes=[("rstd",)])
            P.op("vector", lambda E: E.tensor_tensor(rr32(msq3), msq3, rstd3, ALU.subtract),
                 reads=[("msq",), ("rstd",)], writes=[("msq",)])
            P.op("scalar", lambda E: E.activation(rr32(rstd3), msq3, AF.Ln, bias=epsc[0:1, :], scale=1.0),
                 reads=[("msq",), ("epsc",)], writes=[("rstd",)])
            P.op("scalar", lambda E: E.activation(rr32(rstd3), rstd3, AF.Exp, bias=0.0, scale=-0.5),
                 reads=[("rstd",)], writes=[("rstd",)])
            P.op("vector", lambda E: E.tensor_tensor(rr32(nm3), mu3, rstd3, ALU.mult),
                 reads=[("mu",), ("rstd",)], writes=[("nm",)])
            P.op("vector", lambda E: E.tensor_scalar_mul(rr32(nm3), nm3, -1.0),
                 reads=[("nm",)], writes=[("nm",)])
            P.op("tensor", lambda E: E.matmul(pb[2][:, 0:3], onesf1[:, :], rstd3, start=True, stop=True),
                 reads=[("rstd",), ("onesf1",)], writes=[("pb2",)])
            P.op("tensor", lambda E: E.matmul(pb[3][:, 0:3], onesf1[:, :], nm3, start=True, stop=True),
                 reads=[("nm",), ("onesf1",)], writes=[("pb3",)])
            for h in range(NCH):
                xv3 = xhal[:, h * 3:(h + 1) * 3]
                dst = lnxb[:, h * LNX_STRIDE + 1: h * LNX_STRIDE + 4]
                P.op("vector", lambda E, xv3=xv3: E.tensor_tensor(f2[:, 0:3], xv3, pb[2][:, 0:3], ALU.mult),
                     reads=[("xhal",), ("pb2",)], writes=[("f2",)])
                P.op("vector", lambda E, dst=dst: E.tensor_tensor(dst, f2[:, 0:3], pb[3][:, 0:3], ALU.add),
                     reads=[("f2",), ("pb3",)], writes=[("lnxH", h)])
        if phases >= 1:
            emit_ln_tail(*x_src, *lnx_dst, 0, LNB[0])
            emit_ln_tail(*x_src, *lnx_dst, 1, LNB[1])

        if phases >= 3:
            lnxdup_all = [("lnxdup", c, s2) for c in range(NCH) for s2 in range(NSUB)]

            def norm_rows(kin, bank, which):
                rr = rrn[which]
                krr = ("rr", which)
                nr = bank_rows[0]
                P.op("vector", lambda E, rr=rr, bank=bank, nr=nr: E.tensor_single_scalar(
                    rr32(rr[0:nr, :]), bank[0:nr, :], EPS2, ALU.max),
                    reads=[kin], writes=[krr])
                P.op("scalar", lambda E, rr=rr, nr=nr: E.activation(rr32(rr[0:nr, :]), rr[0:nr, :], AF.Ln, bias=0.0, scale=1.0),
                     reads=[krr], writes=[krr])
                P.op("scalar", lambda E, rr=rr, nr=nr: E.activation(rr32(rr[0:nr, :]), rr[0:nr, :], AF.Exp, bias=0.0, scale=-0.5),
                     reads=[krr], writes=[krr])
                return rr, krr

            bank_rows = [1]

        if phases >= 4:
            cpA = qb.bitcast(F32)[:, 0:SUB]
            cpB = gb.bitcast(F32)[:, 0:SUB]
            for h in range(NCH):
                for s2 in range(NSUB):
                    rk = [("lnx", h, s2), ("lnxH", h), ("cw",)]
                    if s2 == 1:
                        rk.append(("lnx", h, 0))
                    P.op("vector", lambda E, h=h, s2=s2, cpA=cpA: E.tensor_scalar(
                        cpA, lnx_view(h, s2, k=0), cw_sb[:, h * KC: h * KC + 1], None, ALU.mult),
                        reads=rk, writes=[("cpA",)])
                    cur, kcur, oth, koth = cpA, ("cpA",), cpB, ("cpB",)
                    for k in (1, 2):
                        P.op("vector", lambda E, h=h, s2=s2, k=k, cur=cur, oth=oth: E.scalar_tensor_tensor(
                            oth, lnx_view(h, s2, k=k), cw_sb[:, h * KC + k: h * KC + k + 1], cur,
                            ALU.mult, ALU.add),
                            reads=rk + [kcur], writes=[koth])
                        cur, kcur, oth, koth = oth, koth, cur, kcur
                    P.op("vector", lambda E, h=h, s2=s2, cur=cur: E.scalar_tensor_tensor(
                        xcv_view(h, s2), lnx_view(h, s2, k=3), cw_sb[:, h * KC + 3: h * KC + 4], cur,
                        ALU.mult, ALU.add),
                        reads=rk + [kcur], writes=[("xconvraw", h, s2)])

            for h in range(NCH):
                for s2 in range(NSUB):
                    P.op("scalar", lambda E, h=h, s2=s2: E.activation(
                        xcv_view(h, s2), xcv_view(h, s2), AF.Silu, bias=cb_sb[:, h: h + 1], scale=1.0),
                        reads=[("xconvraw", h, s2), ("cb",)], writes=[("xconv", h, s2)])

        if phases >= 5:
            lnx_all = [("lnx", c, s2) for c in range(NCH) for s2 in range(NSUB)]
            xconv_all = [("xconv", c, s2) for c in range(NCH) for s2 in range(NSUB)]
            pend = [None]

            def flush_b1c():
                h = pend[0]
                if h is None:
                    return
                pend[0] = None
                P.op("tensor", lambda E: E.matmul(
                    pb[2][:, 0:1], gsb[:, :], mkc[:, :], start=True, stop=True),
                    reads=[("gsb",), ("mkc",), ("pb2",)], writes=[("pb2",)])
                P.op("vector", lambda E, h=h: E.tensor_copy(carrym[:, h:h + 1], pb[2][:, 0:1]),
                     reads=[("pb2",)], writes=[("carrym", h)])
                for s2 in range(NSUB):
                    Ss = f3 if s2 == 0 else ssc
                    kSs = ("f3",) if s2 == 0 else ("ssc",)
                    bq, kbq = bcbank[(s2, "q")]
                    if s2 == 0:
                        P.op("vector", lambda E, h=h: E.tensor_scalar(
                            f1[:, :], f0[:, :], carrym[:, h:h + 1], None, ALU.mult),
                            reads=[("f0",), ("carrym", h)], writes=[("f1",)])
                    else:
                        P.op("vector", lambda E, h=h: E.tensor_tensor(
                            ctmp[:, :], carrym[:, h:h + 1], f0[:, SUB - 1:SUB], ALU.mult),
                            reads=[("carrym", h), ("f0",)], writes=[("ctmp",)])
                        P.op("vector", lambda E: E.tensor_scalar(
                            f1[:, :], f0[:, :], ctmp[:, :], None, ALU.mult),
                            reads=[("f0",), ("ctmp",)], writes=[("f1",)])
                    P.op("vector", lambda E, Ss=Ss: E.tensor_tensor(Ss[:, :], Ss[:, :], f1[:, :], ALU.add),
                         reads=[kSs, ("f1",)], writes=[kSs])
                    P.op("vector", lambda E, bq=bq, Ss=Ss: E.tensor_tensor(f1[:, :], Ss[:, :], bq[:, :], ALU.mult),
                         reads=[kSs, kbq], writes=[("f1",)])
                    P.op("gpsimd", lambda E, s2=s2: E.tensor_tensor(
                        sq0[:, :], f1[:, :], qb[:, s2 * SUB:(s2 + 1) * SUB], ALU.mult),
                        reads=[("f1",), ("qb", s2)], writes=[("sq0",)])
                    P.op("gpsimd", lambda E, h=h, s2=s2: E.tensor_tensor(
                        slotA_view(h, s2), sq0[:, :], gb[:, s2 * SUB:(s2 + 1) * SUB], ALU.mult),
                        reads=[("sq0",), ("gbt", s2)], writes=[("attnb", h, s2)])

            cur_w = [wload(d_wq[0]), wload(d_wk[0]), wload(d_wv[0]), wload(d_wg[0])]
            for h in range(NCH):
                (wvq, kwq), (wvk, kwk), (wvv, kwv), (wvg, kwg) = cur_w

                def proj(wv, kw, bank_i, rhs_view, rds_extra, s):
                    pbank, kbank = pb[bank_i], (f"pb{bank_i}",)

                    def mms(E, wv=wv, s=s, pbank=pbank, rhs_view=rhs_view):
                        for c in range(NCH):
                            mm = E.matmul(pbank[:, :], wv[:, c * 128:(c + 1) * 128], rhs_view(c, s),
                                          start=(c == 0), stop=(c == NCH - 1), skip_group_check=True)
                        return mm
                    return P.op("tensor", mms, reads=[kw] + rds_extra, writes=[kbank])

                for s in range(NSUB):
                    proj(wvq, kwq, s, lambda c, s2: lnx_view(c, s2),
                         lnx_all if (h == 0 and s == 0) else [], s)
                flush_b1c()
                for s in range(NSUB):
                    extra_q = P.retire("cpA", "cpB") if (h == 0 and s == 0) else []
                    P.op("scalar", lambda E, s=s: E.copy(qb[:, s * SUB:(s + 1) * SUB], pb[s][:, :]),
                         reads=[(f"pb{s}",)], writes=[("qb", s)], extra=extra_q)
                    P.op("scalar", lambda E, s=s: E.square(sq0[:, :], pb[s][:, :]),
                         reads=[(f"pb{s}",)], writes=[("sq0",)])
                    nb = pb[4] if s == 0 else pb[5]
                    knb = ("pb4",) if s == 0 else ("pb5",)
                    P.op("tensor", lambda E, nb=nb: E.matmul(nb[0:2, :], V0, sq0[:, :], start=True, stop=False,
                                                             skip_group_check=True),
                         reads=[("sq0",), ("c3",)], writes=[knb])
                for s in range(NSUB):
                    proj(wvk, kwk, 2 + s, lambda c, s2: lnx_view(c, s2), [], s)
                for s in range(NSUB):
                    nb = pb[4] if s == 0 else pb[5]
                    knb = ("pb4",) if s == 0 else ("pb5",)
                    P.op("scalar", lambda E, s=s: E.square(sq1[:, :], pb[2 + s][:, :]),
                         reads=[(f"pb{2 + s}",)], writes=[("sq1",)])
                    P.op("tensor", lambda E, nb=nb: E.matmul(nb[0:2, :], V1, sq1[:, :], start=False, stop=True,
                                                             skip_group_check=True),
                         reads=[("sq1",)], writes=[knb])
                for s in range(NSUB):
                    proj(wvv, kwv, s, lambda c, s2: lnx_view(c, s2), [], s)
                if h + 1 < NCH:
                    nxt_w = [wload(d_wq[h + 1]), wload(d_wk[h + 1]), wload(d_wv[h + 1])]
                for s in range(NSUB):
                    P.op("vector", lambda E, s=s: E.tensor_copy(kb[:, s * SUB:(s + 1) * SUB], pb[2 + s][:, :]),
                         reads=[(f"pb{2 + s}",)], writes=[("kb", s)])
                rrs = {}
                for s in range(NSUB):
                    nb = pb[4] if s == 0 else pb[5]
                    knb = ("pb4",) if s == 0 else ("pb5",)
                    bank_rows[0] = 2
                    rrs[s] = norm_rows(knb, nb, s)
                bcbank = {(0, "q"): (pb[6], ("pb6",)), (0, "k"): (pb[7], ("pb7",)),
                          (1, "q"): (pb[4], ("pb4",)), (1, "k"): (pb[5], ("pb5",))}
                for s in range(NSUB):
                    rr, krr = rrs[s]
                    bq, kbq = bcbank[(s, "q")]
                    bk, kbk = bcbank[(s, "k")]
                    P.op("tensor", lambda E, rr=rr, bq=bq: E.matmul(bq[:, :], rr32(selq), rr32(rr), start=True, stop=True),
                         reads=[krr, ("t2x",)], writes=[kbq])
                    P.op("tensor", lambda E, rr=rr, bk=bk: E.matmul(bk[:, :], rr32(selk), rr32(rr), start=True, stop=True),
                         reads=[krr], writes=[kbk])
                for s in range(NSUB):
                    if s == 0:
                        proj(wvg, kwg, 2, xcv_view, xconv_all, 0)
                    else:
                        proj(wvg, kwg, 3, xcv_view, [], 1)
                    P.op("vector", lambda E, s=s: E.tensor_copy(vb[:, s * SUB:(s + 1) * SUB], pb[s][:, :]),
                         reads=[(f"pb{s}",)], writes=[("vb", s)])
                for s in range(NSUB):
                    P.op("scalar", lambda E, s=s, h=h: E.activation(
                        gb[:, s * SUB:(s + 1) * SUB], pb[2 + s][:, :], AF.Exp, bias=gb2[:, h:h + 1], scale=-1.0),
                        reads=[(f"pb{2 + s}",), ("gb2",)], writes=[("gbt", s)])
                    P.op("scalar", lambda E, s=s: E.activation(
                        gb[:, s * SUB:(s + 1) * SUB], gb[:, s * SUB:(s + 1) * SUB], AF.Ln, bias=onec[:, 0:1], scale=1.0),
                        reads=[("gbt", s), ("onec",)], writes=[("gbt", s)])
                    P.op("scalar", lambda E, s=s: E.activation(
                        gb[:, s * SUB:(s + 1) * SUB], gb[:, s * SUB:(s + 1) * SUB], AF.Exp, bias=0.0, scale=-1.0),
                        reads=[("gbt", s)], writes=[("gbt", s)])
                    P.op("vector", lambda E, s=s: E.tensor_tensor(sq1[:, :], kb[:, s * SUB:(s + 1) * SUB], vb[:, s * SUB:(s + 1) * SUB], ALU.mult),
                         reads=[("kb", s), ("vb", s)], writes=[("sq1",)])
                    bk, kbk = bcbank[(s, "k")]
                    P.op("vector", lambda E, bk=bk: E.tensor_tensor(f2[:, :], sq1[:, :], bk[:, :], ALU.mult),
                         reads=[("sq1",), kbk], writes=[("f2",)])
                    Ss = f3 if s == 0 else ssc
                    kSs = ("f3",) if s == 0 else ("ssc",)
                    if s == 0:
                        P.op("vector", lambda E, h=h, Ss=Ss: E.tensor_tensor_scan(
                            Ss[:, :], gbc[:, h:h + 1].broadcast_to((128, SUB)), f2[:, :], 0.0,
                            ALU.mult, ALU.add),
                            reads=[("f2",), ("gbc",)], writes=[kSs])
                        P.op("vector", lambda E, Ss=Ss: E.tensor_copy(ccol[:, :], Ss[:, SUB - 1:SUB]),
                             reads=[kSs], writes=[("ccol",)])
                        P.op("vector", lambda E, h=h: E.tensor_tensor_scan(
                            f0[:, :], gbc[:, h:h + 1].broadcast_to((128, SUB)),
                            onec[:, 0:1].broadcast_to((128, SUB)), 1.0,
                            ALU.mult, ALU.bypass),
                            reads=[("gbc",), ("onec",)], writes=[("f0",)])
                    else:
                        P.op("vector", lambda E, h=h, Ss=Ss: E.tensor_tensor_scan(
                            Ss[:, :], gbc[:, h:h + 1].broadcast_to((128, SUB)), f2[:, :], ccol[:, :],
                            ALU.mult, ALU.add),
                            reads=[("f2",), ("gbc",), ("ccol",)], writes=[kSs])
                P.op("vector", lambda E: E.tensor_copy(ccar[:, :], ssc[:, SUB - 1:SUB]),
                     reads=[("ssc",)], writes=[("ccar",)])
                P.dma("gpsimd", lambda E, h=h: E.dma_start(out=d_cin[h], in_=ccar[:, :]),
                      reads=[("ccar",)])
                P.op("gpsimd", lambda E, h=h: E.collective_compute(
                    "AllGather", ALU.bypass, replica_groups=[list(range(8))],
                    ins=[d_cin[h].opt()], outs=[d_gout[h].opt()]),
                    extra=[("GLD", P.cnt["GLD"])])
                P.dma("gpsimd", lambda E, h=h: E.dma_start(
                    out=gsb[:, :],
                    in_=d_gout[h].rearrange("(r p) one -> r (p one)", p=128)),
                    writes=[("gsb",)], extra=[("GP", P.cnt["GP"])])
                pend[0] = h
                if h + 1 < NCH:
                    cur_w = [nxt_w[0], nxt_w[1], nxt_w[2], wload(d_wg[h + 1])]

            flush_b1c()
        if phases >= 6:
            attn_all = [("attnb", c, s2) for c in range(NCH) for s2 in range(NSUB)]
            first_yb = [True]
            stat_banks = {0: (pb[4], pb[5], ("pb4",), ("pb5",)),
                          1: (pb[6], pb[7], ("pb6",), ("pb7",))}
            for h in range(NCH):
                wv, kw = wload(d_wo[h])
                for s in range(NSUB):
                    bi = 2 * (h % 2) + s
                    pbank, kbank = pb[bi], (f"pb{bi}",)

                    def mms(E, wv=wv, s=s, pbank=pbank):
                        for c in range(NCH):
                            mm = E.matmul(pbank[:, :], wv[:, c * 128:(c + 1) * 128], slotA_view(c, s),
                                          start=(c == 0), stop=(c == NCH - 1), skip_group_check=True)
                        return mm
                    P.op("tensor", mms, reads=[kw] + (attn_all if h == 0 and s == 0 else []),
                         writes=[kbank])
                    extra = P.retire("lnx", "lnxH") if first_yb[0] else []
                    first_yb[0] = False
                    P.op("scalar", lambda E, h=h, s=s, pbank=pbank: E.copy(yb_view(h, s), pbank[:, :]),
                         reads=[kbank], writes=[("yb", h, s)], extra=extra)
                    sqt = sq0 if s == 0 else sq1
                    ksq = ("sq0",) if s == 0 else ("sq1",)
                    P.op("scalar", lambda E, pbank=pbank, sqt=sqt: E.square(sqt[:, :], pbank[:, :]),
                         reads=[kbank], writes=[ksq])
                    sb, qbk, ksb, kqb = stat_banks[s]
                    P.op("tensor", lambda E, h=h, s=s, sb=sb, first=(h == 0), last=(h == NCH - 1):
                         E.matmul(sb[0:1, :], Vsum, yb_view(h, s), start=first, stop=last,
                                  skip_group_check=True),
                         reads=[("yb", h, s), ("c3",)], writes=[ksb])
                    P.op("tensor", lambda E, qbk=qbk, sqt=sqt, first=(h == 0), last=(h == NCH - 1):
                         E.matmul(qbk[0:1, :], Vsum, sqt[:, :], start=first, stop=last,
                                  skip_group_check=True),
                         reads=[ksq], writes=[kqb])
            first_ln2 = [True]
            LN3B = {0: (pb[0], pb[1], ("pb0",), ("pb1",)), 1: (pb[2], pb[3], ("pb2",), ("pb3",))}
            for s in range(NSUB):
                sb, qbk, ksb, kqb = stat_banks[s]
                rstd, nm, krstd, knm = ln_rows(ksb, kqb, sb, qbk)
                P.op("tensor", lambda E, rstd=rstd, sb=sb: E.matmul(sb[:, :], rr32(ones1), rr32(rstd), start=True, stop=True),
                     reads=[krstd, ("t2x",)], writes=[ksb])
                P.op("tensor", lambda E, nm=nm, qbk=qbk: E.matmul(qbk[:, :], rr32(ones1), rr32(nm), start=True, stop=True),
                     reads=[knm], writes=[kqb])
                P.op("scalar", lambda E, sb=sb: E.copy(f2[:, :], sb[:, :]), reads=[ksb], writes=[("f2",)])
                P.op("scalar", lambda E, qbk=qbk: E.copy(f3[:, :], qbk[:, :]), reads=[kqb], writes=[("f3",)])
                cpA_ln = qb.bitcast(F32)[:, 0:SUB]
                b3sum, b3sq, k3sum, k3sq = LN3B[s]
                for h in range(NCH):
                    eng = "gpsimd" if h % 3 == 2 else "vector"
                    tmp = f1 if eng == "vector" else cpA_ln
                    ktmp = ("f1",) if eng == "vector" else ("cpA",)
                    extra = []
                    if eng == "gpsimd" and first_ln2[0]:
                        extra = P.retire("qb", "gbt")
                        first_ln2[0] = False
                    P.op(eng, lambda E, h=h, s=s, tmp=tmp: E.tensor_tensor(
                        tmp[:, :], yb_view(h, s), f2[:, :], ALU.mult),
                        reads=[("yb", h, s), ("f2",)], writes=[ktmp], extra=extra)
                    P.op(eng, lambda E, tmp=tmp: E.tensor_tensor(tmp[:, :], tmp[:, :], f3[:, :], ALU.add),
                         reads=[ktmp, ("f3",)], writes=[ktmp])
                    P.op("vector", lambda E, h=h, tmp=tmp: E.tensor_scalar(
                        tmp[:, :], tmp[:, :], lnwc[:, h:h + 1], lnbc[:, h:h + 1], ALU.mult, ALU.add),
                        reads=[ktmp, ("lnwc",), ("lnbc",)], writes=[ktmp])
                    P.op("vector", lambda E, h=h, s=s, tmp=tmp: E.tensor_tensor(
                        x_view(h, s), x_view(h, s), tmp[:, :], ALU.add),
                        reads=[ktmp, ("x", h, s)], writes=[("x", h, s)])
                    xb3 = sq0 if h % 2 == 0 else sq1
                    kxb3 = ("sq0",) if h % 2 == 0 else ("sq1",)
                    xq3 = (f0 if h % 2 == 0 else f1).bitcast(BF16)[:, 0:SUB]
                    kxq3 = ("f0",) if h % 2 == 0 else ("f1",)
                    P.op("scalar", lambda E, h=h, s=s, xb3=xb3: E.copy(xb3[:, :], x_view(h, s)),
                         reads=[("x", h, s)], writes=[kxb3])
                    P.op("vector", lambda E, h=h, s=s, xq3=xq3: E.tensor_tensor(xq3, x_view(h, s), x_view(h, s), ALU.mult),
                         reads=[("x", h, s)], writes=[kxq3])
                    P.op("tensor", lambda E, xb3=xb3, b3sum=b3sum, first=(h == 0), last=(h == NCH - 1):
                         E.matmul(b3sum[0:1, :], Vsum, xb3[:, :], start=first, stop=last,
                                  skip_group_check=True),
                         reads=[kxb3, ("c3",)], writes=[k3sum])
                    P.op("tensor", lambda E, xq3=xq3, b3sq=b3sq, first=(h == 0), last=(h == NCH - 1):
                         E.matmul(b3sq[0:1, :], Vsum, xq3, start=first, stop=last,
                                  skip_group_check=True),
                         reads=[kxq3], writes=[k3sq])

            for s in range(NSUB):
                emit_ln_tail(lambda h, s2: x_view(h, s2), lambda h, s2: ("x", h, s2),
                             lambda h, s2: lnx2_view(h, s2), lambda h, s2: ("lnx2", h, s2),
                             s, LN3B[s])

        if phases >= 8:
            lnx2_all = [("lnx2", c, s2) for c in range(NCH) for s2 in range(NSUB)]
            for s in range(NSUB):
                dve_rx = [fb.bitcast(BF16)[:, 0:SUB] for fb in (f0, f1, f2, f3)]
                dve_keys = [("f0",), ("f1",), ("f2",), ("f3",)]
                gp_rx = [qb[:, 0:SUB], qb[:, SUB:NTOK], gb[:, 0:SUB], gb[:, SUB:NTOK]]
                gp_keys = [("gpx", 0), ("gpx", 1), ("gpx", 2), ("gpx", 3)]
                gcount = [0]
                qbgb_retired = [False]

                def emit_group(grp):
                    gi = gcount[0]
                    gcount[0] += 1
                    eng = "vector" if gi % 4 != 3 else "gpsimd"
                    r = dve_rx if eng == "vector" else gp_rx
                    rk = dve_keys if eng == "vector" else gp_keys
                    extra = []
                    if eng == "gpsimd" and not qbgb_retired[0]:
                        extra = P.retire("qb", "gbt", "cpA", "cpB")
                        qbgb_retired[0] = True
                    c0, c1, c2, c3_ = grp
                    kk = [("hsb", c) for c in grp]
                    P.op(eng, lambda E, r=r, c0=c0, c1=c1: E.tensor_tensor(r[0], hsb_view(c0), hsb_view(c1), ALU.add),
                         reads=[kk[0], kk[1]], writes=[rk[0]], extra=extra)
                    P.op(eng, lambda E, r=r, c0=c0, c1=c1: E.tensor_tensor(r[1], hsb_view(c0), hsb_view(c1), ALU.subtract),
                         reads=[kk[0], kk[1]], writes=[rk[1]])
                    P.op(eng, lambda E, r=r, c2=c2, c3_=c3_: E.tensor_tensor(r[2], hsb_view(c2), hsb_view(c3_), ALU.add),
                         reads=[kk[2], kk[3]], writes=[rk[2]])
                    P.op(eng, lambda E, r=r, c2=c2, c3_=c3_: E.tensor_tensor(r[3], hsb_view(c2), hsb_view(c3_), ALU.subtract),
                         reads=[kk[2], kk[3]], writes=[rk[3]])
                    P.op(eng, lambda E, r=r, c0=c0: E.tensor_tensor(hsb_view(c0), r[0], r[2], ALU.add),
                         reads=[rk[0], rk[2]], writes=[kk[0]])
                    P.op(eng, lambda E, r=r, c2=c2: E.tensor_tensor(hsb_view(c2), r[0], r[2], ALU.subtract),
                         reads=[rk[0], rk[2]], writes=[kk[2]])
                    P.op(eng, lambda E, r=r, c1=c1: E.tensor_tensor(hsb_view(c1), r[1], r[3], ALU.add),
                         reads=[rk[1], rk[3]], writes=[kk[1]])
                    P.op(eng, lambda E, r=r, c3_=c3_: E.tensor_tensor(hsb_view(c3_), r[1], r[3], ALU.subtract),
                         reads=[rk[1], rk[3]], writes=[kk[3]])

                bsched = dict((f_, []) for f_ in range(FCH))
                for g_ in range(16):
                    bsched[4 * g_ + 3].append([4 * g_, 4 * g_ + 1, 4 * g_ + 2, 4 * g_ + 3])
                for b_ in range(FCH):
                    if (b_ // 4) % 4 == 0:
                        bsched[b_ - b_ % 4 + 15].append([b_, b_ + 4, b_ + 8, b_ + 12])
                for b_ in range(16):
                    bsched[63].append([b_, b_ + 16, b_ + 32, b_ + 48])
                bqueue = []
                w13_cur = [wload(d_w1[0]), wload(d_w3[0])]
                for f in range(FCH):
                    (wv1, kw1), (wv3, kw3) = w13_cur
                    pg, kpg = pb[0 + f % 2], (f"pb{0 + f % 2}",)
                    pl, kpl = pb[2 + f % 2], (f"pb{2 + f % 2}",)

                    def mg(E, s=s, wv1=wv1, pg=pg):
                        for c in range(NCH):
                            mm = E.matmul(pg[:, :], wv1[:, c * 128:(c + 1) * 128], lnx2_view(c, s),
                                          start=(c == 0), stop=(c == NCH - 1), skip_group_check=True)
                        return mm

                    def ml(E, s=s, wv3=wv3, pl=pl):
                        for c in range(NCH):
                            mm = E.matmul(pl[:, :], wv3[:, c * 128:(c + 1) * 128], lnx2_view(c, s),
                                          start=(c == 0), stop=(c == NCH - 1), skip_group_check=True)
                        return mm
                    P.op("tensor", mg, reads=[kw1] + (lnx2_all if (f == 0 and s == 0) else []), writes=[kpg])
                    P.op("tensor", ml, reads=[kw3], writes=[kpl])
                    if f + 1 < FCH:
                        w13_cur = [wload(d_w1[f + 1]), wload(d_w3[f + 1])]
                    sg = sq0 if f % 2 == 0 else sq1
                    ksg = ("sq0",) if f % 2 == 0 else ("sq1",)
                    P.op("scalar", lambda E, pg=pg, sg=sg: E.activation(sg[:, :], pg[:, :], AF.Silu, bias=0.0, scale=1.0),
                         reads=[kpg], writes=[ksg])
                    hb = (f0 if f % 2 == 0 else f1).bitcast(BF16)[:, 0:SUB]
                    khb = ("f0",) if f % 2 == 0 else ("f1",)
                    P.op("vector", lambda E, pl=pl, sg=sg, hb=hb: E.tensor_tensor(hb, sg[:, :], pl[:, :], ALU.mult),
                         reads=[ksg, kpl], writes=[khb])
                    pf = pb[4] if f % 2 == 0 else pb[5]
                    kpf = ("pb4",) if f % 2 == 0 else ("pb5",)
                    P.op("tensor", lambda E, pf=pf, hb=hb: E.matmul(
                        pf[:, :], h128_sb[:, :], hb, start=True, stop=True),
                        reads=[khb, ("h128",)], writes=[kpf])
                    extra = []
                    if s == 0 and f == 0:
                        extra = P.retire("attnb")
                    if s == 0 and f == 32:
                        extra = P.retire("yb")
                    P.op("scalar", lambda E, f=f, pf=pf: E.copy(hsb_view(f), pf[:, :]),
                         reads=[kpf], writes=[("hsb", f)], extra=extra)
                    bqueue.extend(bsched[f])
                    if bqueue:
                        emit_group(bqueue.pop(0))
                    if f >= 32 and bqueue:
                        emit_group(bqueue.pop(0))
                w2units = [(d_, q_) for d_ in range(NCH) for q_ in range(4)]
                w2loaded = {}

                def w2load(i):
                    d_, q_ = w2units[i]
                    w2loaded[i] = wload(d_w2[d_][:, q_ * 2048:(q_ + 1) * 2048])
                w2load(0)
                w2load(1)
                while bqueue:
                    emit_group(bqueue.pop(0))
                hsb_all = [("hsb", c) for c in range(FCH)]
                for dchunk in range(NCH):
                    po = pb[6] if dchunk % 2 == 0 else pb[7]
                    kpo = ("pb6",) if dchunk % 2 == 0 else ("pb7",)
                    for quarter in range(4):
                        ui = dchunk * 4 + quarter
                        wv, kw = w2loaded.pop(ui)

                        def mw2(E, s=s, quarter=quarter, wv=wv, po=po):
                            for ci in range(16):
                                pos = quarter * 16 + ci
                                c = (pos // 4) + 16 * (pos % 4)
                                mm = E.matmul(po[:, :], wv[:, ci * 128:(ci + 1) * 128], hsb_view(c),
                                              start=(pos == 0), stop=(pos == FCH - 1), skip_group_check=True)
                            return mm
                        qgrp = [("hsb", (quarter * 16 + ci) // 4 + 16 * ((quarter * 16 + ci) % 4)) for ci in range(16)]
                        P.op("tensor", mw2,
                             reads=[kw] + (qgrp if dchunk == 0 else []),
                             writes=[kpo])
                        if ui + 2 < len(w2units):
                            w2load(ui + 2)
                    P.op("vector", lambda E, dchunk=dchunk, s=s, po=po: E.tensor_tensor(
                        x_view(dchunk, s), x_view(dchunk, s), po[:, :], ALU.add),
                        reads=[kpo, ("x", dchunk, s)], writes=[("x", dchunk, s)])
                    if s == 1:
                        P.store(lambda E, dchunk=dchunk: E.dma_start(
                            out=d_out[dchunk], in_=x_sb[:, dchunk * NTOK:(dchunk + 1) * NTOK]),
                            reads=[("x", dchunk, 0), ("x", dchunk, 1)])

        if debug:
            alldeps = []
            for k, rec in P.track.items():
                if rec[0] is not None:
                    alldeps.append(rec[0])
                alldeps.extend(rec[1])
            P.store(lambda E: E.dma_start(out=dbg_x, in_=x_sb[:, :]), extra=alldeps)
            P.store(lambda E: E.dma_start(out=dbg_lnxb, in_=lnxb[:, :]))
            P.store(lambda E: E.dma_start(out=dbg_xcv, in_=xcv[:, :]))
            P.store(lambda E: E.dma_start(out=dbg_slotA, in_=slotA[:, :]))
            P.store(lambda E: E.dma_start(out=dbg_carrym, in_=carrym[:, :]))
            P.store(lambda E: E.dma_start(out=dbg_gbc, in_=gbc[:, :]))
            P.store(lambda E: E.dma_start(out=dbg_kb, in_=kb[:, 0:SUB]))
            P.store(lambda E: E.dma_start(out=dbg_vb, in_=vb[:, 0:SUB]))
            P.store(lambda E: E.dma_start(out=dbg_f2, in_=f2[:, :]))
            P.store(lambda E: E.dma_start(out=dbg_f3, in_=f3[:, :]))
            P.store(lambda E: E.dma_start(out=dbg_R, in_=R[:, :]))
        if P.cnt["ST"]:
            P.ops["sync"].append(lambda E, n=P.cnt["ST"]: E.wait_ge(sems["ST"], n))

        with nc.allow_low_precision(reason="bf16 activation stores of fp32 accumulations; tolerances validated"), nc.Block() as block:
            @block.tensor
            def _(E):
                for fn_ in P.ops["tensor"]:
                    fn_(E)

            @block.vector
            def _(E):
                for fn_ in P.ops["vector"]:
                    fn_(E)

            @block.scalar
            def _(E):
                for fn_ in P.ops["scalar"]:
                    fn_(E)

            @block.gpsimd
            def _(E):
                for fn_ in P.ops["gpsimd"]:
                    fn_(E)

            @block.sync
            def _(E):
                for fn_ in P.ops["sync"]:
                    fn_(E)

    import os
    if os.environ.get("KERNEL_VERBOSE"):
        print("op counts:", {e: len(P.ops[e]) for e in ENGINES},
              "sem counts:", P.cnt)
    return nc


def stage_inputs(inputs):
    x = np.asarray(inputs["L0_attn_x"], dtype=np.float32)
    conv_w = np.asarray(inputs["L0_attn_conv_w"], dtype=np.float32)
    conv_b = np.asarray(inputs["L0_attn_conv_b"], dtype=np.float32)
    gate_b = np.asarray(inputs["L0_attn_gate_b"], dtype=np.float32)
    ln_w = np.asarray(inputs["L0_attn_ln_w"], dtype=np.float32)
    ln_b = np.asarray(inputs["L0_attn_ln_b"], dtype=np.float32)
    gamma = np.asarray(inputs["L0_attn_gamma"], dtype=np.float32)

    shared = dict(
        wq=_stage_w(inputs["L0_attn_Wq"], NCH, NCH).reshape(NCH, 128, C),
        wk=_stage_w(inputs["L0_attn_Wk"], NCH, NCH).reshape(NCH, 128, C),
        wv=_stage_w(inputs["L0_attn_Wv"], NCH, NCH).reshape(NCH, 128, C),
        wg=_stage_w(inputs["L0_attn_gate_w"], NCH, NCH).reshape(NCH, 128, C),
        wo=_stage_w(inputs["L0_attn_Wo"], NCH, NCH).reshape(NCH, 128, C),
        w1=_stage_w(inputs["L0_ffn_W1"], FCH, NCH).reshape(FCH, 128, C),
        w3=_stage_w(inputs["L0_ffn_W3"], FCH, NCH).reshape(FCH, 128, C),
        w2=np.ascontiguousarray(_stage_w(inputs["L0_ffn_W2"], NCH, FCH)[
            :, :, [b + 16 * j for b in range(16) for j in range(4)], :]).reshape(NCH, 128, HF),
        cwv=np.ascontiguousarray(conv_w[:, 0, :].reshape(NCH, 128, KC).transpose(1, 0, 2)).reshape(128, NCH * KC),
        cbv=np.ascontiguousarray(conv_b.reshape(NCH, 128).T),
        gbv=np.ascontiguousarray(gate_b.reshape(NCH, 128).T),
        lnw=np.ascontiguousarray(ln_w.reshape(NCH, 128).T),
        lnb=np.ascontiguousarray(ln_b.reshape(NCH, 128).T),
        gam=np.ascontiguousarray(gamma.reshape(1, NCH)),
    )
    t2x = np.zeros((2, 256), dtype=np.float32)
    t2x[0, 0:128] = 1.0
    t2x[1, 128:256] = 1.0
    shared["t2x"] = t2x
    c3 = np.zeros((128, 4), dtype=ml_dtypes.bfloat16)
    c3[:, 1] = 1.0
    shared["c3"] = c3
    shared["h128"] = (_hadamard(128) / np.sqrt(HF)).astype(ml_dtypes.bfloat16)

    in_maps = []
    zeros_halo = np.zeros((NCH, 128, 3), dtype=np.float32)
    for core in range(8):
        b = core // 2
        half = core % 2
        xc = np.ascontiguousarray(x[b].T).reshape(NCH, 128, T)
        x16 = np.ascontiguousarray(xc[:, :, half * NTOK:(half + 1) * NTOK])
        m = dict(shared)
        m["x16"] = x16
        if half == 1:
            m["xhalo"] = np.ascontiguousarray(xc[:, :, NTOK - 3:NTOK])
        else:
            m["xhalo"] = zeros_halo
        cm = np.zeros((8, 1), dtype=np.float32)
        if half == 1:
            cm[core - 1, 0] = 1.0
        m["cmask"] = cm
        in_maps.append(m)
    return in_maps


def kernel(**inputs):
    in_maps = stage_inputs(inputs)
    nc = build_nc()
    res = run_bass_kernel_spmd(nc, in_maps, core_ids=list(range(8)))
    out = np.empty((B, T, C), dtype=np.float32)
    for core in range(8):
        b = core // 2
        half = core % 2
        o = np.asarray(res.results[core]["out16"])
        out[b, half * NTOK:(half + 1) * NTOK, :] = o.reshape(C, NTOK).T
    return out


# version 15
# speedup vs baseline: 1.1288x; 1.1288x over previous
import numpy as np
import ml_dtypes
import concourse.bass as bass
import concourse.mybir as mybir
from concourse.bass_utils import run_bass_kernel_spmd

F32 = mybir.dt.float32
BF16 = mybir.dt.bfloat16
AF = mybir.ActivationFunctionType
ALU = mybir.AluOpType

B, T, C = 4, 2048, 2048
NCH = 16
FCH = 64
HF = 8192
NTOK = 1024
SUB = 512
NSUB = 2
KC = 4
LNEPS = 1e-5
EPS2 = 1e-24
LNX_STRIDE = 1028

ENGINES = ("tensor", "vector", "scalar", "gpsimd", "sync")
SEM_OF = {"tensor": "PE", "vector": "DV", "scalar": "SC", "gpsimd": "GP", "sync": "SP"}
DMA_SEM_OF = {"gpsimd": "GLD", "sync": "SLD"}


def _hadamard(n):
    i = np.arange(n, dtype=np.int64)
    x = i[:, None] & i[None, :]
    pc = np.zeros((n, n), dtype=np.int64)
    while x.any():
        pc += x & 1
        x >>= 1
    return np.where(pc % 2 == 0, 1.0, -1.0)


def _stage_w(W, oc, ic):
    W = np.asarray(W, dtype=np.float32)
    return np.ascontiguousarray(W.reshape(oc, 128, ic, 128).transpose(0, 3, 2, 1))


class Plan:
    def __init__(self):
        self.ops = {e: [] for e in ENGINES}
        self.cnt = {s: 0 for s in ("PE", "DV", "SC", "GP", "SP", "GLD", "SLD", "ST")}
        self.track = {}

    def _waits_for(self, reads, writes, extra):
        w = {}

        def add(d):
            if d is None:
                return
            s, v = d
            if s in ("SLD", "GX", "GLD"):
                v = self.cnt[s]
            w[s] = max(w.get(s, 0), v)
        for k in reads:
            rec = self.track.get(k)
            if rec:
                add(rec[0])
        for k in writes:
            rec = self.track.get(k)
            if rec:
                add(rec[0])
                for d in rec[1]:
                    add(d)
        for d in extra:
            add(d)
        return w

    def _commit(self, dep, reads, writes):
        for k in reads:
            rec = self.track.setdefault(k, [None, []])
            rec[1].append(dep)
        for k in writes:
            self.track[k] = [dep, []]

    def retire(self, *prefixes):
        deps = []
        dead = []
        for k, rec in self.track.items():
            if any(k[0] == p for p in prefixes):
                if rec[0] is not None:
                    deps.append(rec[0])
                deps.extend(rec[1])
                dead.append(k)
        for k in dead:
            del self.track[k]
        return deps

    @staticmethod
    def _psum_exclusive(reads, writes):
        rd, wr = [], list(writes)
        for k in reads:
            (wr if (isinstance(k[0], str) and k[0].startswith("pb")) else rd).append(k)
        return rd, wr

    def op(self, eng, fn, reads=(), writes=(), extra=()):
        reads, writes = self._psum_exclusive(reads, writes)
        sem = SEM_OF[eng]
        w = self._waits_for(reads, writes, extra)
        wt = tuple(sorted(w.items()))
        self.cnt[sem] += 1
        dep = (sem, self.cnt[sem])

        def run(E, _fn=fn, _wt=wt, _sem=sem, _self=self):
            for (s, v) in _wt:
                E.wait_ge(_self.sems[s], v)
            _fn(E).then_inc(_self.sems[_sem], 1)
        self.ops[eng].append(run)
        self._commit(dep, reads, writes)
        return dep

    def dma(self, eng, fn, reads=(), writes=(), extra=(), sem=None):
        sem = sem or DMA_SEM_OF[eng]
        w = self._waits_for(reads, writes, extra)
        wt = tuple(sorted(w.items()))
        self.cnt[sem] += 16
        dep = (sem, self.cnt[sem])

        def run(E, _fn=fn, _wt=wt, _sem=sem, _self=self):
            for (s, v) in _wt:
                E.wait_ge(_self.sems[s], v)
            _fn(E).then_inc(_self.sems[_sem], 16)
        self.ops[eng].append(run)
        self._commit(dep, reads, writes)
        return dep

    def store(self, fn, reads=(), extra=()):
        w = self._waits_for(reads, (), extra)
        wt = tuple(sorted(w.items()))
        self.cnt["ST"] += 16
        dep = ("ST", self.cnt["ST"])

        def run(E, _fn=fn, _wt=wt, _self=self):
            for (s, v) in _wt:
                E.wait_ge(_self.sems[s], v)
            _fn(E).then_inc(_self.sems["ST"], 16)
        self.ops["sync"].append(run)
        self._commit(dep, reads, ())
        return dep


def build_nc(phases=99, p3_chunks=NCH, p3_depth=99, debug=False):
    nc = bass.Bass("TRN2", target_bir_lowering=False)

    d_x = nc.dram_tensor("x16", [NCH, 128, NTOK], F32, kind="ExternalInput").ap()
    d_xdup = nc.dram_tensor("xdup", [NSUB, NCH, 128, SUB], F32, kind="ExternalInput").ap()
    d_wq = nc.dram_tensor("wq", [NCH, 128, C], F32, kind="ExternalInput").ap()
    d_wk = nc.dram_tensor("wk", [NCH, 128, C], F32, kind="ExternalInput").ap()
    d_wv = nc.dram_tensor("wv", [NCH, 128, C], F32, kind="ExternalInput").ap()
    d_wg = nc.dram_tensor("wg", [NCH, 128, C], F32, kind="ExternalInput").ap()
    d_wo = nc.dram_tensor("wo", [NCH, 128, C], F32, kind="ExternalInput").ap()
    d_w1 = nc.dram_tensor("w1", [FCH, 128, C], F32, kind="ExternalInput").ap()
    d_w3 = nc.dram_tensor("w3", [FCH, 128, C], F32, kind="ExternalInput").ap()
    d_w2 = nc.dram_tensor("w2", [NCH, 128, HF], F32, kind="ExternalInput").ap()
    d_cwv = nc.dram_tensor("cwv", [128, NCH * KC], F32, kind="ExternalInput").ap()
    d_cbv = nc.dram_tensor("cbv", [128, NCH], F32, kind="ExternalInput").ap()
    d_gbv = nc.dram_tensor("gbv", [128, NCH], F32, kind="ExternalInput").ap()
    d_lnw = nc.dram_tensor("lnw", [128, NCH], F32, kind="ExternalInput").ap()
    d_lnb = nc.dram_tensor("lnb", [128, NCH], F32, kind="ExternalInput").ap()
    d_gam = nc.dram_tensor("gam", [1, NCH], F32, kind="ExternalInput").ap()
    d_t2x = nc.dram_tensor("t2x", [2, 256], F32, kind="ExternalInput").ap()
    d_c3 = nc.dram_tensor("c3", [128, 4], BF16, kind="ExternalInput").ap()
    d_h128 = nc.dram_tensor("h128", [128, 128], BF16, kind="ExternalInput").ap()
    d_out = nc.dram_tensor("out16", [NCH, 128, NTOK], F32, kind="ExternalOutput").ap()
    if debug:
        dbg_x = nc.dram_tensor("dbg_x", [128, NCH * NTOK], F32, kind="ExternalOutput").ap()
        dbg_lnxb = nc.dram_tensor("dbg_lnxb", [128, NCH * LNX_STRIDE], BF16, kind="ExternalOutput").ap()
        dbg_xcv = nc.dram_tensor("dbg_xcv", [128, NCH * NTOK], BF16, kind="ExternalOutput").ap()
        dbg_slotA = nc.dram_tensor("dbg_slotA", [128, NCH * NTOK], BF16, kind="ExternalOutput").ap()
        dbg_carrym = nc.dram_tensor("dbg_carrym", [128, NCH], F32, kind="ExternalOutput").ap()
        dbg_gbc = nc.dram_tensor("dbg_gbc", [128, NCH], F32, kind="ExternalOutput").ap()
        dbg_kb = nc.dram_tensor("dbg_kb", [128, SUB], BF16, kind="ExternalOutput").ap()
        dbg_vb = nc.dram_tensor("dbg_vb", [128, SUB], BF16, kind="ExternalOutput").ap()
        dbg_f2 = nc.dram_tensor("dbg_f2", [128, SUB], F32, kind="ExternalOutput").ap()
        dbg_f3 = nc.dram_tensor("dbg_f3", [128, SUB], F32, kind="ExternalOutput").ap()
        dbg_R = nc.dram_tensor("dbg_R", [128, 2048], F32, kind="ExternalOutput").ap()

    P = Plan()

    from contextlib import ExitStack
    with ExitStack() as ctx:
        ent = ctx.enter_context
        x_sb = ent(nc.sbuf_tensor("x_sb", [128, NCH * NTOK], F32))
        lnxb = ent(nc.sbuf_tensor("lnxb", [128, NCH * LNX_STRIDE], BF16))
        xcv = ent(nc.sbuf_tensor("xcv", [128, NCH * NTOK], BF16))
        slotA = ent(nc.sbuf_tensor("slotA", [128, NCH * NTOK], BF16))
        qb = ent(nc.sbuf_tensor("qb", [128, NTOK], BF16))
        gb = ent(nc.sbuf_tensor("gb", [128, NTOK], BF16))
        kb = ent(nc.sbuf_tensor("kb", [128, NTOK], BF16))
        vb = ent(nc.sbuf_tensor("vb", [128, NTOK], BF16))
        sq0 = ent(nc.sbuf_tensor("sq0", [128, SUB], BF16))
        sq1 = ent(nc.sbuf_tensor("sq1", [128, SUB], BF16))
        f0 = ent(nc.sbuf_tensor("f0", [128, SUB], F32))
        f1 = ent(nc.sbuf_tensor("f1", [128, SUB], F32))
        f2 = ent(nc.sbuf_tensor("f2", [128, SUB], F32))
        f3 = ent(nc.sbuf_tensor("f3", [128, SUB], F32))
        gbc = ent(nc.sbuf_tensor("gbc", [128, NCH], F32))
        gb2 = ent(nc.sbuf_tensor("gb2", [128, NCH], F32))
        cw_sb = ent(nc.sbuf_tensor("cw_sb", [128, NCH * KC], F32))
        cb_sb = ent(nc.sbuf_tensor("cb_sb", [128, NCH], F32))
        carrym = ent(nc.sbuf_tensor("carrym", [128, NCH], F32))
        ccol = ent(nc.sbuf_tensor("ccol", [128, 1], F32))
        epsc = ent(nc.sbuf_tensor("epsc", [128, 1], F32))
        gtmp = ent(nc.sbuf_tensor("gtmp", [1, NCH], F32))
        onec = ent(nc.sbuf_tensor("onec", [128, 1], F32))
        onesf1 = ent(nc.sbuf_tensor("onesf1", [1, 128], F32))
        R = ent(nc.sbuf_tensor("R", [128, 2048], F32))
        lnwc = ent(nc.sbuf_tensor("lnwc", [128, NCH], F32))
        lnbc = ent(nc.sbuf_tensor("lnbc", [128, NCH], F32))
        t2x_sb = ent(nc.sbuf_tensor("t2x_sb", [2, 256], mybir.dt.float32r))
        c3_sb = ent(nc.sbuf_tensor("c3_sb", [128, 4], BF16))
        h128_sb = ent(nc.sbuf_tensor("h128_sb", [128, 128], BF16))
        wpool = ent(nc.sbuf_tensor("wpool", [128, 4 * 2048], BF16))
        pb = [ent(nc.psum_tensor(f"pb{i}", [128, SUB], F32)) for i in range(8)]

        sems = {}
        for s in ("PE", "DV", "SC", "GP", "SP", "GLD", "SLD", "ST",
                  "GW0", "GW1", "GW2", "GW3", "GX"):
            sems[s] = ent(nc.semaphore(s))
            P.cnt.setdefault(s, 0)
        P.sems = sems

        r_mu = R[0:1, 0:512]
        r_msq = R[0:1, 512:1024]
        r_rstd = R[0:1, 1024:1536]
        r_nm = R[0:1, 1536:2048]
        rrn = [R[0:2, 1024:1536], R[0:2, 1536:2048]]

        def rr32(ap):
            return ap.bitcast(mybir.dt.float32r)

        ones1 = t2x_sb[0:1, 0:128]
        selq = t2x_sb[:, 0:128]
        selk = t2x_sb[:, 128:256]
        Vsum = c3_sb[:, 1:2]
        V0 = c3_sb[:, 1:3]
        V1 = c3_sb[:, 0:2]

        def lnx_view(h, s, k=3, w=SUB):
            base = h * LNX_STRIDE + 4 + s * SUB - 3 + k
            return lnxb[:, base: base + w]

        def yb_view(h, s):
            return lnxb[:, h * NTOK + s * SUB: h * NTOK + (s + 1) * SUB]

        def lnx2_view(h, s):
            return xcv[:, h * NTOK + s * SUB: h * NTOK + (s + 1) * SUB]

        xcv_view = lnx2_view

        def slotA_view(h, s):
            return slotA[:, h * NTOK + s * SUB: h * NTOK + (s + 1) * SUB]

        def lnxdup_view(h, s):
            return slotA[:, s * (NCH * SUB) + h * SUB: s * (NCH * SUB) + (h + 1) * SUB]

        def dupx_view(h, s):
            return xcv[:, s * (NCH * SUB) + h * SUB: s * (NCH * SUB) + (h + 1) * SUB]

        def hsb_view(f, s=None):
            if f < 32:
                return slotA[:, f * SUB: (f + 1) * SUB]
            return lnxb[:, (f - 32) * SUB: (f - 31) * SUB]

        def x_view(h, s):
            return x_sb[:, h * NTOK + s * SUB: h * NTOK + (s + 1) * SUB]

        wctr = [0]

        def wload(dram_ap):
            slot = wctr[0] % 4
            wctr[0] += 1
            key = ("wsl", slot)
            view = wpool[:, slot * 2048:(slot + 1) * 2048]
            P.dma("gpsimd", lambda E, view=view, dram_ap=dram_ap: E.dma_start(out=view, in_=dram_ap),
                  writes=[key], sem=f"GW{slot}")
            return view, key

        P.dma("sync", lambda E: E.dma_start(
            out=x_sb[:, :].rearrange("p (c t) -> p c t", t=NTOK),
            in_=d_x.rearrange("c p t -> p c t")),
            writes=[("x", h, s) for h in range(NCH) for s in range(NSUB)])
        P.dma("sync", lambda E: E.dma_start(out=cw_sb[:, :], in_=d_cwv), writes=[("cw",)])
        P.dma("sync", lambda E: E.dma_start(out=cb_sb[:, :], in_=d_cbv), writes=[("cb",)])
        P.dma("sync", lambda E: E.dma_start(out=gb2[:, :], in_=d_gbv), writes=[("gb2",)])
        P.dma("sync", lambda E: E.dma_start(out=lnwc[:, :], in_=d_lnw), writes=[("lnwc",)])
        P.dma("sync", lambda E: E.dma_start(out=lnbc[:, :], in_=d_lnb), writes=[("lnbc",)])
        P.dma("gpsimd", lambda E: E.dma_start(out=t2x_sb[:, :], in_=d_t2x), writes=[("t2x",)])
        P.dma("sync", lambda E: E.dma_start(out=c3_sb[:, :], in_=d_c3), writes=[("c3",)])
        P.dma("sync", lambda E: E.dma_start(out=h128_sb[:, :], in_=d_h128), writes=[("h128",)])
        P.dma("sync", lambda E: E.dma_start(out=gtmp[:, :], in_=d_gam), writes=[("gtmp",)])
        P.op("vector", lambda E: E.memset(onesf1[:, :], 1.0), writes=[("onesf1",)])
        P.op("vector", lambda E: E.memset(onec[:, :], 1.0), writes=[("onec",)])
        for s in range(NSUB):
            P.dma("gpsimd", lambda E, s=s: E.dma_start(
                out=xcv[:, s * (NCH * SUB):(s + 1) * (NCH * SUB)].rearrange("p (c t) -> p c t", t=SUB),
                in_=d_xdup[s].rearrange("c p t -> p c t")),
                writes=[("dupx", h, s) for h in range(NCH)], sem="GX")

        P.op("vector", lambda E: E.memset(epsc[:, :], LNEPS), writes=[("epsc",)])
        P.op("tensor", lambda E: E.matmul(pb[4][:, 0:NCH], onesf1[:, :], gtmp[:, :], start=True, stop=True),
             reads=[("onesf1",), ("gtmp",)], writes=[("pb4",)])
        P.op("vector", lambda E: E.tensor_copy(gbc[:, :], pb[4][:, 0:NCH]),
             reads=[("pb4",)], writes=[("gbc",)])
        P.op("vector", lambda E: E.tensor_scalar_mul(gb2[:, :], gb2[:, :], -1.0),
             reads=[("gb2",)], writes=[("gb2",)])

        def ln_rows(bank_sum_key, bank_sq_key, bank_sum, bank_sq, slot=0):
            mu, rstd = r_mu, r_rstd
            msq, nm = r_msq, r_nm
            kmu, krstd = ("mu",), ("rstd",)
            kmsq, knm = ("msq",), ("nm",)
            P.op("vector", lambda E: E.tensor_scalar_mul(rr32(mu), bank_sum[0:1, :], 1.0 / C),
                 reads=[bank_sum_key], writes=[kmu])
            P.op("vector", lambda E: E.tensor_scalar_mul(rr32(msq), bank_sq[0:1, :], 1.0 / C),
                 reads=[bank_sq_key], writes=[kmsq])
            P.op("vector", lambda E: E.tensor_tensor(rr32(rstd), mu, mu, ALU.mult),
                 reads=[kmu], writes=[krstd])
            P.op("vector", lambda E: E.tensor_tensor(rr32(msq), msq, rstd, ALU.subtract),
                 reads=[kmsq, krstd], writes=[kmsq])
            P.op("scalar", lambda E: E.activation(rr32(rstd), msq, AF.Ln, bias=epsc[0:1, :], scale=1.0),
                 reads=[kmsq, ("epsc",)], writes=[krstd])
            P.op("scalar", lambda E: E.activation(rr32(rstd), rstd, AF.Exp, bias=0.0, scale=-0.5),
                 reads=[krstd], writes=[krstd])
            P.op("vector", lambda E: E.tensor_tensor(rr32(nm), mu, rstd, ALU.mult),
                 reads=[kmu, krstd], writes=[knm])
            P.op("vector", lambda E: E.tensor_scalar_mul(rr32(nm), nm, -1.0),
                 reads=[knm], writes=[knm])
            return rstd, nm, krstd, knm

        def emit_ln_stats(src_view, src_key, s, banks, cast=True):
            bsum, bsq, ksum, ksq_ = banks
            for h in range(NCH):
                xv = src_view(h, s)
                xqv = (f0 if h % 2 == 0 else f1).bitcast(BF16)[:, 0:SUB]
                kxq = ("f0",) if h % 2 == 0 else ("f1",)
                if cast:
                    xb = sq0 if h % 2 == 0 else sq1
                    kxb = ("sq0",) if h % 2 == 0 else ("sq1",)
                    P.op("scalar", lambda E, xv=xv, xb=xb: E.copy(xb[:, :], xv),
                         reads=[src_key(h, s)], writes=[kxb])
                    rhs_s, krhs = (lambda xb=xb: xb[:, :])(), kxb
                else:
                    rhs_s, krhs = xv, src_key(h, s)
                P.op("vector", lambda E, xv=xv, xqv=xqv: E.tensor_tensor(xqv, xv, xv, ALU.mult),
                     reads=[src_key(h, s)], writes=[kxq])
                P.op("tensor", lambda E, rhs_s=rhs_s, bsum=bsum, first=(h == 0), last=(h == NCH - 1):
                     E.matmul(bsum[0:1, :], Vsum, rhs_s, start=first, stop=last,
                              skip_group_check=True),
                     reads=[krhs, ("c3",)], writes=[ksum])
                P.op("tensor", lambda E, xqv=xqv, bsq=bsq, first=(h == 0), last=(h == NCH - 1):
                     E.matmul(bsq[0:1, :], Vsum, xqv, start=first, stop=last,
                              skip_group_check=True),
                     reads=[kxq], writes=[ksq_])

        def emit_ln_tail(src_view, src_key, out_view, out_key, s, banks):
            bsum, bsq, ksum, ksq_ = banks
            rstd, nm, krstd, knm = ln_rows(ksum, ksq_, bsum, bsq)
            P.op("tensor", lambda E, rstd=rstd, bsum=bsum: E.matmul(bsum[:, :], rr32(ones1), rr32(rstd), start=True, stop=True),
                 reads=[krstd, ("t2x",)], writes=[ksum])
            P.op("tensor", lambda E, nm=nm, bsq=bsq: E.matmul(bsq[:, :], rr32(ones1), rr32(nm), start=True, stop=True),
                 reads=[knm], writes=[ksq_])
            P.op("scalar", lambda E, bsum=bsum: E.copy(f2[:, :], bsum[:, :]), reads=[ksum], writes=[("f2",)])
            P.op("scalar", lambda E, bsq=bsq: E.copy(f3[:, :], bsq[:, :]), reads=[ksq_], writes=[("f3",)])
            cpA_ln = qb.bitcast(F32)[:, 0:SUB]
            for h in range(NCH):
                xv = src_view(h, s)
                ov = out_view(h, s)
                eng = "gpsimd" if h % 3 == 2 else "vector"
                tmp = f1 if eng == "vector" else cpA_ln
                ktmp = ("f1",) if eng == "vector" else ("cpA",)
                P.op(eng, lambda E, xv=xv, tmp=tmp: E.tensor_tensor(tmp[:, :], xv, f2[:, :], ALU.mult),
                     reads=[src_key(h, s), ("f2",)], writes=[ktmp])
                P.op(eng, lambda E, ov=ov, tmp=tmp: E.tensor_tensor(ov, tmp[:, :], f3[:, :], ALU.add),
                     reads=[ktmp, ("f3",)], writes=[out_key(h, s)])

        LNB = {0: (pb[6], pb[7], ("pb6",), ("pb7",)), 1: (pb[4], pb[5], ("pb4",), ("pb5",))}
        LNB2 = (pb[2], pb[3], ("pb2",), ("pb3",))

        def emit_ln(src_view, src_key, out_view, out_key):
            for s2 in range(NSUB):
                emit_ln_stats(src_view, src_key, s2, LNB[s2])
            for s2 in range(NSUB):
                emit_ln_tail(src_view, src_key, out_view, out_key, s2, LNB[s2])

        x_src = (lambda h, s2: x_view(h, s2), lambda h, s2: ("x", h, s2))
        lnx_dst = (lambda h, s2: lnx_view(h, s2), lambda h, s2: ("lnx", h, s2))
        dup_src = (dupx_view, lambda h, s2: ("dupx", h, s2))
        dup_dst = (lnxdup_view, lambda h, s2: ("lnxdup", h, s2))
        LNB1b = (pb[0], pb[1], ("pb0",), ("pb1",))
        if phases >= 2:
            for s2 in range(NSUB):
                emit_ln_stats(*dup_src, s2, LNB[s2], cast=False)
        if phases >= 1:
            emit_ln_stats(*x_src, 0, LNB2)
            emit_ln_stats(*x_src, 1, LNB1b)
        if phases >= 2:
            for s2 in range(NSUB):
                emit_ln_tail(*dup_src, *dup_dst, s2, LNB[s2])
            for h in range(NCH):
                src = slotA[:, (NCH * SUB) + h * SUB + SUB - 3: (NCH * SUB) + (h + 1) * SUB]
                dst = lnxb[:, h * LNX_STRIDE + 1: h * LNX_STRIDE + 4]
                P.op("vector", lambda E, src=src, dst=dst: E.tensor_copy(dst, src),
                     reads=[("lnxdup", h, 1)], writes=[("lnxH", h)])
        if phases >= 1:
            emit_ln_tail(*x_src, *lnx_dst, 0, LNB2)
            emit_ln_tail(*x_src, *lnx_dst, 1, LNB1b)

        if phases >= 3:
            lnxdup_all = [("lnxdup", c, s2) for c in range(NCH) for s2 in range(NSUB)]

            def norm_rows(kin, bank, which):
                rr = rrn[which]
                krr = ("rr", which)
                nr = bank_rows[0]
                P.op("vector", lambda E, rr=rr, bank=bank, nr=nr: E.tensor_single_scalar(
                    rr32(rr[0:nr, :]), bank[0:nr, :], EPS2, ALU.max),
                    reads=[kin], writes=[krr])
                P.op("scalar", lambda E, rr=rr, nr=nr: E.activation(rr32(rr[0:nr, :]), rr[0:nr, :], AF.Ln, bias=0.0, scale=1.0),
                     reads=[krr], writes=[krr])
                P.op("scalar", lambda E, rr=rr, nr=nr: E.activation(rr32(rr[0:nr, :]), rr[0:nr, :], AF.Exp, bias=0.0, scale=-0.5),
                     reads=[krr], writes=[krr])
                return rr, krr

            bank_rows = [1]
            dup_w = [wload(d_wk[0]), wload(d_wv[0])]
            for h in range(p3_chunks):
                (wvk, kwk), (wvv, kwv) = dup_w
                dup_hoisted = [False]
                for s in range(NSUB):
                    pk, kk = pb[s], (f"pb{s}",)
                    pv, kv_ = pb[2 + s], (f"pb{2 + s}",)

                    def kmms(E, s=s, wvk=wvk, pk=pk):
                        for c in range(NCH):
                            mm = E.matmul(pk[:, :], wvk[:, c * 128:(c + 1) * 128], lnxdup_view(c, s),
                                          start=(c == 0), stop=(c == NCH - 1), skip_group_check=True)
                        return mm
                    if p3_depth >= 1:
                        P.op("tensor", kmms, reads=[kwk] + lnxdup_all, writes=[kk])

                    def vmms(E, s=s, wvv=wvv, pv=pv):
                        for c in range(NCH):
                            mm = E.matmul(pv[:, :], wvv[:, c * 128:(c + 1) * 128], lnxdup_view(c, s),
                                          start=(c == 0), stop=(c == NCH - 1), skip_group_check=True)
                        return mm
                    if p3_depth >= 1:
                        P.op("tensor", vmms, reads=[kwv], writes=[kv_])
                    if s == 1 and h + 1 < p3_chunks and not dup_hoisted[0]:
                        dup_w = [wload(d_wk[h + 1]), wload(d_wv[h + 1])]
                        dup_hoisted[0] = True
                    if p3_depth >= 2:
                        P.op("vector", lambda E, s=s, pk=pk: E.tensor_copy(kb[:, s * SUB:(s + 1) * SUB], pk[:, :]),
                             reads=[kk], writes=[("kb", s)])
                    if p3_depth >= 2:
                        P.op("vector", lambda E, s=s, pv=pv: E.tensor_copy(vb[:, s * SUB:(s + 1) * SUB], pv[:, :]),
                             reads=[kv_], writes=[("vb", s)])
                    if p3_depth >= 2:
                        P.op("scalar", lambda E, pk=pk: E.square(sq0[:, :], pk[:, :]),
                             reads=[kk], writes=[("sq0",)])
                    if p3_depth >= 3:
                        P.op("tensor", lambda E: E.matmul(pb[4][0:1, :], Vsum, sq0[:, :], start=True, stop=True,
                                                          skip_group_check=True),
                             reads=[("sq0",), ("c3",)], writes=[("pb4",)])
                    bank_rows[0] = 1
                    if p3_depth >= 4:
                        rr, krr = norm_rows(("pb4",), pb[4], 0)
                    if p3_depth >= 5:
                        P.op("tensor", lambda E, rr=rr: E.matmul(pb[5][:, :], rr32(ones1), rr32(rr[0:1, :]), start=True, stop=True),
                             reads=[krr, ("t2x",)], writes=[("pb5",)])
                    if p3_depth >= 6:
                        P.op("vector", lambda E, s=s: E.tensor_tensor(sq1[:, :], kb[:, s * SUB:(s + 1) * SUB], vb[:, s * SUB:(s + 1) * SUB], ALU.mult),
                             reads=[("kb", s), ("vb", s)], writes=[("sq1",)])
                        P.op("vector", lambda E: E.tensor_tensor(f2[:, :], sq1[:, :], pb[5][:, :], ALU.mult),
                             reads=[("sq1",), ("pb5",)], writes=[("f2",)])
                    if s == 0 and p3_depth >= 7:
                        P.op("vector", lambda E, h=h: E.tensor_tensor_scan(
                            f3[:, :], gbc[:, h:h + 1].broadcast_to((128, SUB)), f2[:, :], 0.0,
                            ALU.mult, ALU.add),
                            reads=[("f2",), ("gbc",)], writes=[("f3",)])
                        P.op("vector", lambda E: E.tensor_copy(ccol[:, :], f3[:, SUB - 1:SUB]),
                             reads=[("f3",)], writes=[("ccol",)])
                    elif s == 1 and p3_depth >= 7:
                        P.op("vector", lambda E, h=h: E.tensor_tensor_scan(
                            f3[:, :], gbc[:, h:h + 1].broadcast_to((128, SUB)), f2[:, :], ccol[:, :],
                            ALU.mult, ALU.add),
                            reads=[("f2",), ("gbc",), ("ccol",)], writes=[("f3",)])
                        P.op("vector", lambda E, h=h: E.tensor_copy(carrym[:, h:h + 1], f3[:, SUB - 1:SUB]),
                             reads=[("f3",)], writes=[("carrym", h)])
                if s == 1 and phases >= 4:
                    cpA = qb.bitcast(F32)[:, 0:SUB]
                    cpB = gb.bitcast(F32)[:, 0:SUB]
                    for s2 in range(NSUB):
                        rk = [("lnx", h, s2), ("lnxH", h), ("cw",)]
                        if s2 == 1:
                            rk.append(("lnx", h, 0))
                        P.op("vector", lambda E, h=h, s2=s2, cpA=cpA: E.tensor_scalar(
                            cpA, lnx_view(h, s2, k=0), cw_sb[:, h * KC: h * KC + 1], None, ALU.mult),
                            reads=rk, writes=[("cpA",)])
                        cur, kcur, oth, koth = cpA, ("cpA",), cpB, ("cpB",)
                        for k in (1, 2):
                            P.op("vector", lambda E, h=h, s2=s2, k=k, cur=cur, oth=oth: E.scalar_tensor_tensor(
                                oth, lnx_view(h, s2, k=k), cw_sb[:, h * KC + k: h * KC + k + 1], cur,
                                ALU.mult, ALU.add),
                                reads=rk + [kcur], writes=[koth])
                            cur, kcur, oth, koth = oth, koth, cur, kcur
                        extra = P.retire("dupx") if (h == 0 and s2 == 0) else []
                        P.op("vector", lambda E, h=h, s2=s2, cur=cur: E.scalar_tensor_tensor(
                            xcv_view(h, s2), lnx_view(h, s2, k=3), cw_sb[:, h * KC + 3: h * KC + 4], cur,
                            ALU.mult, ALU.add),
                            reads=rk + [kcur], writes=[("xconvraw", h, s2)], extra=extra)

        if phases >= 4:
            for h in range(NCH):
                for s2 in range(NSUB):
                    P.op("scalar", lambda E, h=h, s2=s2: E.activation(
                        xcv_view(h, s2), xcv_view(h, s2), AF.Silu, bias=cb_sb[:, h: h + 1], scale=1.0),
                        reads=[("xconvraw", h, s2), ("cb",)], writes=[("xconv", h, s2)])

        if phases >= 5:
            lnx_all = [("lnx", c, s2) for c in range(NCH) for s2 in range(NSUB)]
            xconv_all = [("xconv", c, s2) for c in range(NCH) for s2 in range(NSUB)]
            first_attnb = [True]
            cur_w = [wload(d_wq[0]), wload(d_wk[0]), wload(d_wv[0]), wload(d_wg[0])]
            for h in range(NCH):
                (wvq, kwq), (wvk, kwk), (wvv, kwv), (wvg, kwg) = cur_w

                def proj(wv, kw, bank_i, rhs_view, rds_extra, s):
                    pbank, kbank = pb[bank_i], (f"pb{bank_i}",)

                    def mms(E, wv=wv, s=s, pbank=pbank, rhs_view=rhs_view):
                        for c in range(NCH):
                            mm = E.matmul(pbank[:, :], wv[:, c * 128:(c + 1) * 128], rhs_view(c, s),
                                          start=(c == 0), stop=(c == NCH - 1), skip_group_check=True)
                        return mm
                    return P.op("tensor", mms, reads=[kw] + rds_extra, writes=[kbank])

                for s in range(NSUB):
                    proj(wvq, kwq, s, lambda c, s2: lnx_view(c, s2),
                         lnx_all if (h == 0 and s == 0) else [], s)
                for s in range(NSUB):
                    extra_q = P.retire("cpA", "cpB") if (h == 0 and s == 0) else []
                    P.op("scalar", lambda E, s=s: E.copy(qb[:, s * SUB:(s + 1) * SUB], pb[s][:, :]),
                         reads=[(f"pb{s}",)], writes=[("qb", s)], extra=extra_q)
                    P.op("scalar", lambda E, s=s: E.square(sq0[:, :], pb[s][:, :]),
                         reads=[(f"pb{s}",)], writes=[("sq0",)])
                    nb = pb[4] if s == 0 else pb[5]
                    knb = ("pb4",) if s == 0 else ("pb5",)
                    P.op("tensor", lambda E, nb=nb: E.matmul(nb[0:2, :], V0, sq0[:, :], start=True, stop=False,
                                                             skip_group_check=True),
                         reads=[("sq0",), ("c3",)], writes=[knb])
                for s in range(NSUB):
                    proj(wvk, kwk, 2 + s, lambda c, s2: lnx_view(c, s2), [], s)
                for s in range(NSUB):
                    nb = pb[4] if s == 0 else pb[5]
                    knb = ("pb4",) if s == 0 else ("pb5",)
                    P.op("scalar", lambda E, s=s: E.square(sq1[:, :], pb[2 + s][:, :]),
                         reads=[(f"pb{2 + s}",)], writes=[("sq1",)])
                    P.op("tensor", lambda E, nb=nb: E.matmul(nb[0:2, :], V1, sq1[:, :], start=False, stop=True,
                                                             skip_group_check=True),
                         reads=[("sq1",)], writes=[knb])
                for s in range(NSUB):
                    proj(wvv, kwv, s, lambda c, s2: lnx_view(c, s2), [], s)
                if h + 1 < NCH:
                    nxt_w = [wload(d_wq[h + 1]), wload(d_wk[h + 1]), wload(d_wv[h + 1])]
                for s in range(NSUB):
                    P.op("vector", lambda E, s=s: E.tensor_copy(kb[:, s * SUB:(s + 1) * SUB], pb[2 + s][:, :]),
                         reads=[(f"pb{2 + s}",)], writes=[("kb", s)])
                rrs = {}
                for s in range(NSUB):
                    nb = pb[4] if s == 0 else pb[5]
                    knb = ("pb4",) if s == 0 else ("pb5",)
                    bank_rows[0] = 2
                    rrs[s] = norm_rows(knb, nb, s)
                bcbank = {(0, "q"): (pb[6], ("pb6",)), (0, "k"): (pb[7], ("pb7",)),
                          (1, "q"): (pb[4], ("pb4",)), (1, "k"): (pb[5], ("pb5",))}
                for s in range(NSUB):
                    rr, krr = rrs[s]
                    bq, kbq = bcbank[(s, "q")]
                    bk, kbk = bcbank[(s, "k")]
                    P.op("tensor", lambda E, rr=rr, bq=bq: E.matmul(bq[:, :], rr32(selq), rr32(rr), start=True, stop=True),
                         reads=[krr, ("t2x",)], writes=[kbq])
                    P.op("tensor", lambda E, rr=rr, bk=bk: E.matmul(bk[:, :], rr32(selk), rr32(rr), start=True, stop=True),
                         reads=[krr], writes=[kbk])
                for s in range(NSUB):
                    if s == 0:
                        proj(wvg, kwg, 2, xcv_view, xconv_all, 0)
                    else:
                        proj(wvg, kwg, 3, xcv_view, [], 1)
                    P.op("vector", lambda E, s=s: E.tensor_copy(vb[:, s * SUB:(s + 1) * SUB], pb[s][:, :]),
                         reads=[(f"pb{s}",)], writes=[("vb", s)])
                for s in range(NSUB):
                    P.op("scalar", lambda E, s=s, h=h: E.activation(
                        gb[:, s * SUB:(s + 1) * SUB], pb[2 + s][:, :], AF.Exp, bias=gb2[:, h:h + 1], scale=-1.0),
                        reads=[(f"pb{2 + s}",), ("gb2",)], writes=[("gbt", s)])
                    P.op("scalar", lambda E, s=s: E.activation(
                        gb[:, s * SUB:(s + 1) * SUB], gb[:, s * SUB:(s + 1) * SUB], AF.Ln, bias=onec[:, 0:1], scale=1.0),
                        reads=[("gbt", s), ("onec",)], writes=[("gbt", s)])
                    P.op("scalar", lambda E, s=s: E.activation(
                        gb[:, s * SUB:(s + 1) * SUB], gb[:, s * SUB:(s + 1) * SUB], AF.Exp, bias=0.0, scale=-1.0),
                        reads=[("gbt", s)], writes=[("gbt", s)])
                    bk, kbk = bcbank[(s, "k")]
                    bq, kbq = bcbank[(s, "q")]
                    P.op("vector", lambda E, s=s: E.tensor_tensor(sq1[:, :], kb[:, s * SUB:(s + 1) * SUB], vb[:, s * SUB:(s + 1) * SUB], ALU.mult),
                         reads=[("kb", s), ("vb", s)], writes=[("sq1",)])
                    P.op("vector", lambda E, bk=bk: E.tensor_tensor(f2[:, :], sq1[:, :], bk[:, :], ALU.mult),
                         reads=[("sq1",), kbk], writes=[("f2",)])
                    if s == 0:
                        P.op("vector", lambda E, h=h: E.tensor_tensor_scan(
                            f3[:, :], gbc[:, h:h + 1].broadcast_to((128, SUB)), f2[:, :], carrym[:, h:h + 1],
                            ALU.mult, ALU.add),
                            reads=[("f2",), ("gbc",), ("carrym", h)], writes=[("f3",)])
                        P.op("vector", lambda E: E.tensor_copy(ccol[:, :], f3[:, SUB - 1:SUB]),
                             reads=[("f3",)], writes=[("ccol",)])
                    else:
                        P.op("vector", lambda E, h=h: E.tensor_tensor_scan(
                            f3[:, :], gbc[:, h:h + 1].broadcast_to((128, SUB)), f2[:, :], ccol[:, :],
                            ALU.mult, ALU.add),
                            reads=[("f2",), ("gbc",), ("ccol",)], writes=[("f3",)])
                    P.op("vector", lambda E, bq=bq: E.tensor_tensor(f1[:, :], f3[:, :], bq[:, :], ALU.mult),
                         reads=[("f3",), kbq], writes=[("f1",)])
                    P.op("gpsimd", lambda E, s=s: E.tensor_tensor(
                        sq0[:, :], f1[:, :], qb[:, s * SUB:(s + 1) * SUB], ALU.mult),
                        reads=[("f1",), ("qb", s)], writes=[("sq0",)])
                    extra = P.retire("lnxdup") if first_attnb[0] else []
                    first_attnb[0] = False
                    P.op("gpsimd", lambda E, h=h, s=s: E.tensor_tensor(
                        slotA_view(h, s), sq0[:, :], gb[:, s * SUB:(s + 1) * SUB], ALU.mult),
                        reads=[("sq0",), ("gbt", s)], writes=[("attnb", h, s)], extra=extra)
                if h + 1 < NCH:
                    cur_w = [nxt_w[0], nxt_w[1], nxt_w[2], wload(d_wg[h + 1])]

        if phases >= 6:
            attn_all = [("attnb", c, s2) for c in range(NCH) for s2 in range(NSUB)]
            first_yb = [True]
            stat_banks = {0: (pb[4], pb[5], ("pb4",), ("pb5",)),
                          1: (pb[6], pb[7], ("pb6",), ("pb7",))}
            for h in range(NCH):
                wv, kw = wload(d_wo[h])
                for s in range(NSUB):
                    bi = 2 * (h % 2) + s
                    pbank, kbank = pb[bi], (f"pb{bi}",)

                    def mms(E, wv=wv, s=s, pbank=pbank):
                        for c in range(NCH):
                            mm = E.matmul(pbank[:, :], wv[:, c * 128:(c + 1) * 128], slotA_view(c, s),
                                          start=(c == 0), stop=(c == NCH - 1), skip_group_check=True)
                        return mm
                    P.op("tensor", mms, reads=[kw] + (attn_all if h == 0 and s == 0 else []),
                         writes=[kbank])
                    extra = P.retire("lnx", "lnxH") if first_yb[0] else []
                    first_yb[0] = False
                    P.op("scalar", lambda E, h=h, s=s, pbank=pbank: E.copy(yb_view(h, s), pbank[:, :]),
                         reads=[kbank], writes=[("yb", h, s)], extra=extra)
                    sqt = sq0 if s == 0 else sq1
                    ksq = ("sq0",) if s == 0 else ("sq1",)
                    P.op("scalar", lambda E, pbank=pbank, sqt=sqt: E.square(sqt[:, :], pbank[:, :]),
                         reads=[kbank], writes=[ksq])
                    sb, qbk, ksb, kqb = stat_banks[s]
                    P.op("tensor", lambda E, h=h, s=s, sb=sb, first=(h == 0), last=(h == NCH - 1):
                         E.matmul(sb[0:1, :], Vsum, yb_view(h, s), start=first, stop=last,
                                  skip_group_check=True),
                         reads=[("yb", h, s), ("c3",)], writes=[ksb])
                    P.op("tensor", lambda E, qbk=qbk, sqt=sqt, first=(h == 0), last=(h == NCH - 1):
                         E.matmul(qbk[0:1, :], Vsum, sqt[:, :], start=first, stop=last,
                                  skip_group_check=True),
                         reads=[ksq], writes=[kqb])
            first_ln2 = [True]
            LN3B = {0: (pb[0], pb[1], ("pb0",), ("pb1",)), 1: (pb[2], pb[3], ("pb2",), ("pb3",))}
            for s in range(NSUB):
                sb, qbk, ksb, kqb = stat_banks[s]
                rstd, nm, krstd, knm = ln_rows(ksb, kqb, sb, qbk)
                P.op("tensor", lambda E, rstd=rstd, sb=sb: E.matmul(sb[:, :], rr32(ones1), rr32(rstd), start=True, stop=True),
                     reads=[krstd, ("t2x",)], writes=[ksb])
                P.op("tensor", lambda E, nm=nm, qbk=qbk: E.matmul(qbk[:, :], rr32(ones1), rr32(nm), start=True, stop=True),
                     reads=[knm], writes=[kqb])
                P.op("scalar", lambda E, sb=sb: E.copy(f2[:, :], sb[:, :]), reads=[ksb], writes=[("f2",)])
                P.op("scalar", lambda E, qbk=qbk: E.copy(f3[:, :], qbk[:, :]), reads=[kqb], writes=[("f3",)])
                cpA_ln = qb.bitcast(F32)[:, 0:SUB]
                b3sum, b3sq, k3sum, k3sq = LN3B[s]
                for h in range(NCH):
                    eng = "gpsimd" if h % 3 == 2 else "vector"
                    tmp = f1 if eng == "vector" else cpA_ln
                    ktmp = ("f1",) if eng == "vector" else ("cpA",)
                    extra = []
                    if eng == "gpsimd" and first_ln2[0]:
                        extra = P.retire("qb", "gbt")
                        first_ln2[0] = False
                    P.op(eng, lambda E, h=h, s=s, tmp=tmp: E.tensor_tensor(
                        tmp[:, :], yb_view(h, s), f2[:, :], ALU.mult),
                        reads=[("yb", h, s), ("f2",)], writes=[ktmp], extra=extra)
                    P.op(eng, lambda E, tmp=tmp: E.tensor_tensor(tmp[:, :], tmp[:, :], f3[:, :], ALU.add),
                         reads=[ktmp, ("f3",)], writes=[ktmp])
                    P.op("vector", lambda E, h=h, tmp=tmp: E.tensor_scalar(
                        tmp[:, :], tmp[:, :], lnwc[:, h:h + 1], lnbc[:, h:h + 1], ALU.mult, ALU.add),
                        reads=[ktmp, ("lnwc",), ("lnbc",)], writes=[ktmp])
                    P.op("vector", lambda E, h=h, s=s, tmp=tmp: E.tensor_tensor(
                        x_view(h, s), x_view(h, s), tmp[:, :], ALU.add),
                        reads=[ktmp, ("x", h, s)], writes=[("x", h, s)])
                    xb3 = sq0 if h % 2 == 0 else sq1
                    kxb3 = ("sq0",) if h % 2 == 0 else ("sq1",)
                    xq3 = (f0 if h % 2 == 0 else f1).bitcast(BF16)[:, 0:SUB]
                    kxq3 = ("f0",) if h % 2 == 0 else ("f1",)
                    P.op("scalar", lambda E, h=h, s=s, xb3=xb3: E.copy(xb3[:, :], x_view(h, s)),
                         reads=[("x", h, s)], writes=[kxb3])
                    P.op("vector", lambda E, h=h, s=s, xq3=xq3: E.tensor_tensor(xq3, x_view(h, s), x_view(h, s), ALU.mult),
                         reads=[("x", h, s)], writes=[kxq3])
                    P.op("tensor", lambda E, xb3=xb3, b3sum=b3sum, first=(h == 0), last=(h == NCH - 1):
                         E.matmul(b3sum[0:1, :], Vsum, xb3[:, :], start=first, stop=last,
                                  skip_group_check=True),
                         reads=[kxb3, ("c3",)], writes=[k3sum])
                    P.op("tensor", lambda E, xq3=xq3, b3sq=b3sq, first=(h == 0), last=(h == NCH - 1):
                         E.matmul(b3sq[0:1, :], Vsum, xq3, start=first, stop=last,
                                  skip_group_check=True),
                         reads=[kxq3], writes=[k3sq])

            for s in range(NSUB):
                emit_ln_tail(lambda h, s2: x_view(h, s2), lambda h, s2: ("x", h, s2),
                             lambda h, s2: lnx2_view(h, s2), lambda h, s2: ("lnx2", h, s2),
                             s, LN3B[s])

        if phases >= 8:
            lnx2_all = [("lnx2", c, s2) for c in range(NCH) for s2 in range(NSUB)]
            for s in range(NSUB):
                dve_rx = [fb.bitcast(BF16)[:, 0:SUB] for fb in (f0, f1, f2, f3)]
                dve_keys = [("f0",), ("f1",), ("f2",), ("f3",)]
                gp_rx = [qb[:, 0:SUB], qb[:, SUB:NTOK], gb[:, 0:SUB], gb[:, SUB:NTOK]]
                gp_keys = [("gpx", 0), ("gpx", 1), ("gpx", 2), ("gpx", 3)]
                gcount = [0]
                qbgb_retired = [False]

                def emit_group(grp):
                    gi = gcount[0]
                    gcount[0] += 1
                    eng = "vector" if gi % 4 != 3 else "gpsimd"
                    r = dve_rx if eng == "vector" else gp_rx
                    rk = dve_keys if eng == "vector" else gp_keys
                    extra = []
                    if eng == "gpsimd" and not qbgb_retired[0]:
                        extra = P.retire("qb", "gbt", "cpA", "cpB")
                        qbgb_retired[0] = True
                    c0, c1, c2, c3_ = grp
                    kk = [("hsb", c) for c in grp]
                    P.op(eng, lambda E, r=r, c0=c0, c1=c1: E.tensor_tensor(r[0], hsb_view(c0), hsb_view(c1), ALU.add),
                         reads=[kk[0], kk[1]], writes=[rk[0]], extra=extra)
                    P.op(eng, lambda E, r=r, c0=c0, c1=c1: E.tensor_tensor(r[1], hsb_view(c0), hsb_view(c1), ALU.subtract),
                         reads=[kk[0], kk[1]], writes=[rk[1]])
                    P.op(eng, lambda E, r=r, c2=c2, c3_=c3_: E.tensor_tensor(r[2], hsb_view(c2), hsb_view(c3_), ALU.add),
                         reads=[kk[2], kk[3]], writes=[rk[2]])
                    P.op(eng, lambda E, r=r, c2=c2, c3_=c3_: E.tensor_tensor(r[3], hsb_view(c2), hsb_view(c3_), ALU.subtract),
                         reads=[kk[2], kk[3]], writes=[rk[3]])
                    P.op(eng, lambda E, r=r, c0=c0: E.tensor_tensor(hsb_view(c0), r[0], r[2], ALU.add),
                         reads=[rk[0], rk[2]], writes=[kk[0]])
                    P.op(eng, lambda E, r=r, c2=c2: E.tensor_tensor(hsb_view(c2), r[0], r[2], ALU.subtract),
                         reads=[rk[0], rk[2]], writes=[kk[2]])
                    P.op(eng, lambda E, r=r, c1=c1: E.tensor_tensor(hsb_view(c1), r[1], r[3], ALU.add),
                         reads=[rk[1], rk[3]], writes=[kk[1]])
                    P.op(eng, lambda E, r=r, c3_=c3_: E.tensor_tensor(hsb_view(c3_), r[1], r[3], ALU.subtract),
                         reads=[rk[1], rk[3]], writes=[kk[3]])

                bsched = dict((f_, []) for f_ in range(FCH))
                for g_ in range(16):
                    bsched[4 * g_ + 3].append([4 * g_, 4 * g_ + 1, 4 * g_ + 2, 4 * g_ + 3])
                for b_ in range(FCH):
                    if (b_ // 4) % 4 == 0:
                        bsched[b_ - b_ % 4 + 15].append([b_, b_ + 4, b_ + 8, b_ + 12])
                for b_ in range(16):
                    bsched[63].append([b_, b_ + 16, b_ + 32, b_ + 48])
                bqueue = []
                w13_cur = [wload(d_w1[0]), wload(d_w3[0])]
                for f in range(FCH):
                    (wv1, kw1), (wv3, kw3) = w13_cur
                    pg, kpg = pb[0 + f % 2], (f"pb{0 + f % 2}",)
                    pl, kpl = pb[2 + f % 2], (f"pb{2 + f % 2}",)

                    def mg(E, s=s, wv1=wv1, pg=pg):
                        for c in range(NCH):
                            mm = E.matmul(pg[:, :], wv1[:, c * 128:(c + 1) * 128], lnx2_view(c, s),
                                          start=(c == 0), stop=(c == NCH - 1), skip_group_check=True)
                        return mm

                    def ml(E, s=s, wv3=wv3, pl=pl):
                        for c in range(NCH):
                            mm = E.matmul(pl[:, :], wv3[:, c * 128:(c + 1) * 128], lnx2_view(c, s),
                                          start=(c == 0), stop=(c == NCH - 1), skip_group_check=True)
                        return mm
                    P.op("tensor", mg, reads=[kw1] + (lnx2_all if (f == 0 and s == 0) else []), writes=[kpg])
                    P.op("tensor", ml, reads=[kw3], writes=[kpl])
                    if f + 1 < FCH:
                        w13_cur = [wload(d_w1[f + 1]), wload(d_w3[f + 1])]
                    sg = sq0 if f % 2 == 0 else sq1
                    ksg = ("sq0",) if f % 2 == 0 else ("sq1",)
                    P.op("scalar", lambda E, pg=pg, sg=sg: E.activation(sg[:, :], pg[:, :], AF.Silu, bias=0.0, scale=1.0),
                         reads=[kpg], writes=[ksg])
                    hb = (f0 if f % 2 == 0 else f1).bitcast(BF16)[:, 0:SUB]
                    khb = ("f0",) if f % 2 == 0 else ("f1",)
                    P.op("vector", lambda E, pl=pl, sg=sg, hb=hb: E.tensor_tensor(hb, sg[:, :], pl[:, :], ALU.mult),
                         reads=[ksg, kpl], writes=[khb])
                    pf = pb[4] if f % 2 == 0 else pb[5]
                    kpf = ("pb4",) if f % 2 == 0 else ("pb5",)
                    P.op("tensor", lambda E, pf=pf, hb=hb: E.matmul(
                        pf[:, :], h128_sb[:, :], hb, start=True, stop=True),
                        reads=[khb, ("h128",)], writes=[kpf])
                    extra = []
                    if s == 0 and f == 0:
                        extra = P.retire("attnb")
                    if s == 0 and f == 32:
                        extra = P.retire("yb")
                    P.op("scalar", lambda E, f=f, pf=pf: E.copy(hsb_view(f), pf[:, :]),
                         reads=[kpf], writes=[("hsb", f)], extra=extra)
                    bqueue.extend(bsched[f])
                    if bqueue:
                        emit_group(bqueue.pop(0))
                    if f >= 32 and bqueue:
                        emit_group(bqueue.pop(0))
                w2units = [(d_, q_) for d_ in range(NCH) for q_ in range(4)]
                w2loaded = {}

                def w2load(i):
                    d_, q_ = w2units[i]
                    w2loaded[i] = wload(d_w2[d_][:, q_ * 2048:(q_ + 1) * 2048])
                w2load(0)
                w2load(1)
                while bqueue:
                    emit_group(bqueue.pop(0))
                hsb_all = [("hsb", c) for c in range(FCH)]
                for dchunk in range(NCH):
                    po = pb[6] if dchunk % 2 == 0 else pb[7]
                    kpo = ("pb6",) if dchunk % 2 == 0 else ("pb7",)
                    for quarter in range(4):
                        ui = dchunk * 4 + quarter
                        wv, kw = w2loaded.pop(ui)

                        def mw2(E, s=s, quarter=quarter, wv=wv, po=po):
                            for ci in range(16):
                                pos = quarter * 16 + ci
                                c = (pos // 4) + 16 * (pos % 4)
                                mm = E.matmul(po[:, :], wv[:, ci * 128:(ci + 1) * 128], hsb_view(c),
                                              start=(pos == 0), stop=(pos == FCH - 1), skip_group_check=True)
                            return mm
                        qgrp = [("hsb", (quarter * 16 + ci) // 4 + 16 * ((quarter * 16 + ci) % 4)) for ci in range(16)]
                        P.op("tensor", mw2,
                             reads=[kw] + (qgrp if dchunk == 0 else []),
                             writes=[kpo])
                        if ui + 2 < len(w2units):
                            w2load(ui + 2)
                    P.op("vector", lambda E, dchunk=dchunk, s=s, po=po: E.tensor_tensor(
                        x_view(dchunk, s), x_view(dchunk, s), po[:, :], ALU.add),
                        reads=[kpo, ("x", dchunk, s)], writes=[("x", dchunk, s)])
                    if s == 1:
                        P.store(lambda E, dchunk=dchunk: E.dma_start(
                            out=d_out[dchunk], in_=x_sb[:, dchunk * NTOK:(dchunk + 1) * NTOK]),
                            reads=[("x", dchunk, 0), ("x", dchunk, 1)])

        if debug:
            alldeps = []
            for k, rec in P.track.items():
                if rec[0] is not None:
                    alldeps.append(rec[0])
                alldeps.extend(rec[1])
            P.store(lambda E: E.dma_start(out=dbg_x, in_=x_sb[:, :]), extra=alldeps)
            P.store(lambda E: E.dma_start(out=dbg_lnxb, in_=lnxb[:, :]))
            P.store(lambda E: E.dma_start(out=dbg_xcv, in_=xcv[:, :]))
            P.store(lambda E: E.dma_start(out=dbg_slotA, in_=slotA[:, :]))
            P.store(lambda E: E.dma_start(out=dbg_carrym, in_=carrym[:, :]))
            P.store(lambda E: E.dma_start(out=dbg_gbc, in_=gbc[:, :]))
            P.store(lambda E: E.dma_start(out=dbg_kb, in_=kb[:, 0:SUB]))
            P.store(lambda E: E.dma_start(out=dbg_vb, in_=vb[:, 0:SUB]))
            P.store(lambda E: E.dma_start(out=dbg_f2, in_=f2[:, :]))
            P.store(lambda E: E.dma_start(out=dbg_f3, in_=f3[:, :]))
            P.store(lambda E: E.dma_start(out=dbg_R, in_=R[:, :]))
        if P.cnt["ST"]:
            P.ops["sync"].append(lambda E, n=P.cnt["ST"]: E.wait_ge(sems["ST"], n))

        with nc.allow_low_precision(reason="bf16 activation stores of fp32 accumulations; tolerances validated"), nc.Block() as block:
            @block.tensor
            def _(E):
                for fn_ in P.ops["tensor"]:
                    fn_(E)

            @block.vector
            def _(E):
                for fn_ in P.ops["vector"]:
                    fn_(E)

            @block.scalar
            def _(E):
                for fn_ in P.ops["scalar"]:
                    fn_(E)

            @block.gpsimd
            def _(E):
                for fn_ in P.ops["gpsimd"]:
                    fn_(E)

            @block.sync
            def _(E):
                for fn_ in P.ops["sync"]:
                    fn_(E)

    import os
    if os.environ.get("KERNEL_VERBOSE"):
        print("op counts:", {e: len(P.ops[e]) for e in ENGINES},
              "sem counts:", P.cnt)
    return nc


def stage_inputs(inputs):
    x = np.asarray(inputs["L0_attn_x"], dtype=np.float32)
    conv_w = np.asarray(inputs["L0_attn_conv_w"], dtype=np.float32)
    conv_b = np.asarray(inputs["L0_attn_conv_b"], dtype=np.float32)
    gate_b = np.asarray(inputs["L0_attn_gate_b"], dtype=np.float32)
    ln_w = np.asarray(inputs["L0_attn_ln_w"], dtype=np.float32)
    ln_b = np.asarray(inputs["L0_attn_ln_b"], dtype=np.float32)
    gamma = np.asarray(inputs["L0_attn_gamma"], dtype=np.float32)

    shared = dict(
        wq=_stage_w(inputs["L0_attn_Wq"], NCH, NCH).reshape(NCH, 128, C),
        wk=_stage_w(inputs["L0_attn_Wk"], NCH, NCH).reshape(NCH, 128, C),
        wv=_stage_w(inputs["L0_attn_Wv"], NCH, NCH).reshape(NCH, 128, C),
        wg=_stage_w(inputs["L0_attn_gate_w"], NCH, NCH).reshape(NCH, 128, C),
        wo=_stage_w(inputs["L0_attn_Wo"], NCH, NCH).reshape(NCH, 128, C),
        w1=_stage_w(inputs["L0_ffn_W1"], FCH, NCH).reshape(FCH, 128, C),
        w3=_stage_w(inputs["L0_ffn_W3"], FCH, NCH).reshape(FCH, 128, C),
        w2=np.ascontiguousarray(_stage_w(inputs["L0_ffn_W2"], NCH, FCH)[
            :, :, [b + 16 * j for b in range(16) for j in range(4)], :]).reshape(NCH, 128, HF),
        cwv=np.ascontiguousarray(conv_w[:, 0, :].reshape(NCH, 128, KC).transpose(1, 0, 2)).reshape(128, NCH * KC),
        cbv=np.ascontiguousarray(conv_b.reshape(NCH, 128).T),
        gbv=np.ascontiguousarray(gate_b.reshape(NCH, 128).T),
        lnw=np.ascontiguousarray(ln_w.reshape(NCH, 128).T),
        lnb=np.ascontiguousarray(ln_b.reshape(NCH, 128).T),
        gam=np.ascontiguousarray(gamma.reshape(1, NCH)),
    )
    t2x = np.zeros((2, 256), dtype=np.float32)
    t2x[0, 0:128] = 1.0
    t2x[1, 128:256] = 1.0
    shared["t2x"] = t2x
    c3 = np.zeros((128, 4), dtype=ml_dtypes.bfloat16)
    c3[:, 1] = 1.0
    shared["c3"] = c3
    shared["h128"] = (_hadamard(128) / np.sqrt(HF)).astype(ml_dtypes.bfloat16)

    in_maps = []
    zeros_dup = np.zeros((NSUB, NCH, 128, SUB), dtype=np.float32)
    for core in range(8):
        b = core // 2
        half = core % 2
        xc = np.ascontiguousarray(x[b].T).reshape(NCH, 128, T)
        x16 = np.ascontiguousarray(xc[:, :, half * NTOK:(half + 1) * NTOK])
        if half == 1:
            xdup = np.ascontiguousarray(
                xc[:, :, 0:NTOK].reshape(NCH, 128, NSUB, SUB).transpose(2, 0, 1, 3))
        else:
            xdup = zeros_dup
        m = dict(shared)
        m["x16"] = x16
        m["xdup"] = xdup
        in_maps.append(m)
    return in_maps


def kernel(**inputs):
    in_maps = stage_inputs(inputs)
    nc = build_nc()
    res = run_bass_kernel_spmd(nc, in_maps, core_ids=list(range(8)))
    out = np.empty((B, T, C), dtype=np.float32)
    for core in range(8):
        b = core // 2
        half = core % 2
        o = np.asarray(res.results[core]["out16"])
        out[b, half * NTOK:(half + 1) * NTOK, :] = o.reshape(C, NTOK).T
    return out
